# version 01
# baseline (speedup 1.0000x reference)
import math
import numpy as np
import ml_dtypes
import concourse.bass as bass
import concourse.mybir as mybir
from concourse.bass_utils import run_bass_kernel_spmd
from concourse.bass_core_rust import IndirectOffsetOnAxis
from contextlib import ExitStack

F32 = mybir.dt.float32
BF16 = mybir.dt.bfloat16
I32 = mybir.dt.int32
AF = mybir.ActivationFunctionType
ALU = mybir.AluOpType
AX = mybir.AxisListType
DynSlice = bass.DynSlice

N_CORES = 8
B = 2
S = 8192
D = 1024
SLOC = S // N_CORES
TLOC = B * SLOC
NTOK = B * S
NCH = TLOC // 128
EPS = 1e-5
CAPACITY = 2560
NEXP = 8

BF = ml_dtypes.bfloat16

ENGINES = ("sync", "gpsimd", "vector", "scalar", "tensor")


class Tok:
    __slots__ = ("sem", "val")

    def __init__(self, sem, val):
        self.sem = sem
        self.val = val


class Prog:

    def __init__(self, nc, dma_slots=6):
        self.nc = nc
        self.ops = {e: [] for e in ENGINES}
        self.prog_sem = {}
        self.dma_sems = {}
        self.dma_rr = {e: 0 for e in ENGINES}
        self.prog_count = {e: 0 for e in ENGINES}
        self.dma_slots = dma_slots
        self._dma_counts = {}
        self._ctx = None

    def __enter__(self):
        self._ctx = ExitStack()
        E = self._ctx.enter_context
        for e in ENGINES:
            self.prog_sem[e] = E(self.nc.semaphore(f"pg_{e}"))
            self.dma_sems[e] = [E(self.nc.semaphore(f"dq_{e}{i}"))
                                for i in range(self.dma_slots)]
            for s in self.dma_sems[e]:
                self._dma_counts[id(s)] = 0
        return self

    def __exit__(self, *a):
        return self._ctx.__exit__(*a)

    def comp(self, eng, fn, waits=(), name=""):
        self.prog_count[eng] += 1
        tok = Tok(self.prog_sem[eng], self.prog_count[eng])
        self.ops[eng].append(("comp", fn, tuple(w for w in waits if w), tok, name))
        return tok

    def raw(self, eng, fn, waits=(), name=""):
        self.ops[eng].append(("raw", fn, tuple(w for w in waits if w), None, name))
        return None

    def dma(self, eng, fn, waits=(), name=""):
        slot = self.dma_rr[eng] % self.dma_slots
        self.dma_rr[eng] += 1
        sem = self.dma_sems[eng][slot]
        prev = self._dma_counts[id(sem)]
        self._dma_counts[id(sem)] = prev + 16
        tok = Tok(sem, prev + 16)
        th = Tok(sem, prev) if prev else None
        ws = tuple(w for w in waits if w) + ((th,) if th else ())
        self.ops[eng].append(("dma", fn, ws, tok, name))
        return tok

    def cc(self, fn, waits=(), name=""):
        return self.comp("gpsimd", fn, waits, name)

    def barrier_tokens(self):
        toks = []
        for e in ENGINES:
            if self.prog_count[e]:
                toks.append(Tok(self.prog_sem[e], self.prog_count[e]))
            for s in self.dma_sems[e]:
                c = self._dma_counts[id(s)]
                if c:
                    toks.append(Tok(s, c))
        return toks

    def emit(self):
        nc = self.nc
        with nc.Block() as block:
            for ename in ENGINES:
                ops = self.ops[ename]
                if not ops:
                    continue

                def mk(ops=ops):
                    def body(eng):
                        hw = {}
                        for kind, fn, waits, tok, name in ops:
                            for w in waits:
                                if hw.get(id(w.sem), -1) >= w.val:
                                    continue
                                eng.wait_ge(w.sem, w.val)
                                hw[id(w.sem)] = w.val
                            inst = fn(eng)
                            if tok is not None:
                                amt = 16 if kind == "dma" else 1
                                assert inst is not None, f"op {name} returned None"
                                inst.then_inc(tok.sem, amt)
                    return body

                getattr(block, ename)(mk())


def new_nc():
    return bass.Bass("TRN2", target_bir_lowering=False)


def sb(E, nc, name, shape, dtype):
    return E(nc.sbuf_tensor(name, shape, dtype))


def ps(E, nc, name, shape, dtype=F32):
    return E(nc.psum_tensor(name, shape, dtype))


class ConstPool:

    def __init__(self, p, nc, E, n=24):
        self.t = sb(E, nc, "cpool", [128, n], F32)
        self.p = p
        self.nc = nc
        self.k = 0
        self.toks = []

    def reg(self, value):
        col = self.t[:, self.k:self.k + 1]
        self.k += 1
        tok = self.p.comp("gpsimd", lambda e, col=col, value=value: e.memset(col, value))
        self.nc.const_aps.register(F32, float(value), col)
        self.toks.append(tok)
        return tok


def build_identity(p, nc, E):
    ident = sb(E, nc, "ident", [128, 128], BF16)
    t0 = p.comp("gpsimd", lambda e: e.memset(ident[:], 1.0))
    t1 = p.comp("gpsimd", lambda e: e.affine_select(
        ident[:], ident[:], pattern=[[1, 128]], compare_op=ALU.is_equal,
        fill=0.0, base=0, channel_multiplier=-1), waits=[t0])
    return ident, t1


def cast_load_weight(p, dram_ap, sbuf_chunks, waits=(), rows=None, eng="gpsimd"):
    toks = []
    R = rows if rows is not None else dram_ap.shape[0]
    for i, t in enumerate(sbuf_chunks):
        r0 = i * 128
        r1 = min(r0 + 128, R)
        toks.append(p.dma(eng, lambda e, t=t, r0=r0, r1=r1: e.dma_start(
            out=t[0:r1 - r0, :], in_=dram_ap[r0:r1, :]), waits=waits))
    return toks



def build_launch_A():
    nc = new_nc()
    x_in = nc.dram_tensor("x", [TLOC, D], F32, kind="ExternalInput").ap()
    wq_in = nc.dram_tensor("wq", [D, D], F32, kind="ExternalInput").ap()
    wk_in = nc.dram_tensor("wk", [D, D], F32, kind="ExternalInput").ap()
    wv_in = nc.dram_tensor("wv", [D, D], F32, kind="ExternalInput").ap()

    q_out = nc.dram_tensor("q_fm", [D, TLOC], BF16, kind="ExternalOutput").ap()
    k_out = nc.dram_tensor("k_fm", [D, TLOC], BF16, kind="ExternalOutput").ap()
    v_out = nc.dram_tensor("v_tm", [TLOC, D], BF16, kind="ExternalOutput").ap()
    kn_out = nc.dram_tensor("knorm", [1, TLOC], F32, kind="ExternalOutput").ap()
    qm_out = nc.dram_tensor("qmaxsq", [1, 1], F32, kind="ExternalOutput").ap()

    with ExitStack() as ctx, Prog(nc) as p:
        E = ctx.enter_context
        x_sb = [sb(E, nc, f"x{i}", [128, D], F32) for i in range(2)]
        xh_tm = [sb(E, nc, f"xh{i}", [128, D], BF16) for i in range(2)]
        xh_fm = [sb(E, nc, f"xf{d}", [128, TLOC], BF16) for d in range(8)]
        w_sb = {w: [sb(E, nc, f"{w}{d}", [128, D], BF16) for d in range(8)]
                for w in ("wq", "wk", "wv")}
        stat = sb(E, nc, "stat", [128, 8 * NCH], F32)
        sq_scr = [sb(E, nc, f"sqs{i}", [128, D], F32) for i in range(2)]
        qk_buf = [sb(E, nc, f"qk{j}", [128, TLOC], BF16) for j in range(2)]
        sqb_scr = [sb(E, nc, f"sqb{j}", [128, 512], BF16) for j in range(2)]
        ones_col = sb(E, nc, "ones", [128, 1], BF16)
        v_tm = [sb(E, nc, f"vt{i}", [128, D], BF16) for i in range(NCH)]
        nrm_row = sb(E, nc, "nrm", [1, TLOC], F32)
        qsq_row = sb(E, nc, "qsq", [1, TLOC], F32)
        qmax_sb = sb(E, nc, "qmx", [1, 1], F32)
        ident, t_id = build_identity(p, nc, E)
        cpool = ConstPool(p, nc, E)
        t_eps = cpool.reg(EPS)

        ps_t = [ps(E, nc, f"pst{i}", [128, 128], BF16) for i in range(2)]
        ps_mm = [ps(E, nc, f"psm{i}", [128, 512], F32) for i in range(2)]
        ps_n = ps(E, nc, "psn", [1, TLOC], F32)

        t_w = {w: cast_load_weight(p, {"wq": wq_in, "wk": wk_in, "wv": wv_in}[w],
                                   w_sb[w]) for w in ("wq", "wk", "wv")}
        t_on = p.comp("gpsimd", lambda e: e.memset(ones_col[:], 1.0))

        t_fm = [[None] * 8 for _ in range(NCH)]
        t_xh = []
        for i in range(NCH):
            xb = x_sb[i % 2]
            hb = xh_tm[i % 2]
            free_w = [t_fm[i - 2][7]] if i >= 2 else []
            t_xi = p.dma("sync", lambda e, i=i, xb=xb: e.dma_start(
                out=xb[:], in_=x_in[i * 128:(i + 1) * 128, :]), waits=free_w)
            scr = sq_scr[i % 2]
            c0 = i * 8
            st_ = lambda j, c0=c0: stat[:, c0 + j:c0 + j + 1]
            t_s1 = p.comp("vector", lambda e, xb=xb, st_=st_: e.reduce_sum(
                st_(0), xb[:], axis=AX.X), waits=[t_xi])
            t_sq = p.comp("scalar", lambda e, xb=xb, scr=scr, st_=st_: e.activation(
                scr[:], xb[:], AF.Square, accum_out=st_(1)),
                waits=[t_xi])
            t_m = p.comp("vector", lambda e, st_=st_: e.tensor_scalar(
                st_(2), st_(0), 1.0 / D, None, ALU.mult),
                waits=[t_s1])
            t_v1 = p.comp("vector", lambda e, st_=st_: e.tensor_tensor(
                st_(3), st_(2), st_(2), ALU.mult), waits=[t_m])
            t_v2 = p.comp("vector", lambda e, st_=st_: e.tensor_scalar(
                st_(4), st_(1), 1.0 / D, st_(3),
                ALU.mult, ALU.subtract), waits=[t_sq, t_v1])
            t_sd = p.comp("scalar", lambda e, st_=st_: e.activation(
                st_(5), st_(4), AF.Sqrt, bias=EPS, scale=1.0),
                waits=[t_v2, t_eps])
            t_r = p.comp("vector", lambda e, st_=st_: e.reciprocal(
                st_(6), st_(5)), waits=[t_sd])
            t_h = p.comp("vector", lambda e, xb=xb, hb=hb, st_=st_: e.tensor_scalar(
                hb[:], xb[:], st_(2), st_(6),
                ALU.subtract, ALU.mult), waits=[t_r, t_xi])
            t_xh.append(t_h)
            for d in range(8):
                pt = ps_t[(i * 8 + d) % 2]
                prev2 = None
                if i * 8 + d >= 2:
                    j = i * 8 + d - 2
                    prev2 = t_fm[j // 8][j % 8]
                t_tr = p.comp("tensor", lambda e, d=d, hb=hb, pt=pt: e.transpose(
                    pt[:], hb[:, d * 128:(d + 1) * 128], ident[:]),
                    waits=[t_h, t_id] + ([prev2] if prev2 else []))
                t_ev = p.comp("vector", lambda e, i=i, d=d, pt=pt: e.tensor_copy(
                    xh_fm[d][:, i * 128:(i + 1) * 128], pt[:]), waits=[t_tr])
                t_fm[i][d] = t_ev
        t_fm_all = [t_fm[i][d] for i in range(NCH) for d in range(8)]

        bank_free = {}

        def proj_fm_stream(wname, out_dram, scale, psn_dst, after=()):
            toks_sq = [(None, t) for t in after] if after else []
            last_ec = []
            for ec in range(8):
                buf = qk_buf[ec % 2]
                for tt in range(4):
                    bank = ps_mm[tt % 2]
                    mm_toks = []
                    for d in range(8):
                        w0 = [t_w[wname][d], t_fm_all[-1]]
                        if d == 0 and id(bank) in bank_free:
                            w0.append(bank_free[id(bank)])
                        t_mm = p.comp("tensor", lambda e, ec=ec, tt=tt, d=d, bank=bank, wname=wname: e.matmul(
                            bank[:], w_sb[wname][d][:, ec * 128:(ec + 1) * 128],
                            xh_fm[d][:, tt * 512:(tt + 1) * 512],
                            start=(d == 0), stop=(d == 7)),
                            waits=w0)
                        mm_toks.append(t_mm)
                    bw = [mm_toks[-1]]
                    if ec >= 2:
                        bw.append(last_ec[ec - 2])
                    t_e = p.comp("scalar", lambda e, tt=tt, bank=bank, buf=buf, scale=scale: e.activation(
                        buf[:, tt * 512:(tt + 1) * 512], bank[:], AF.Copy,
                        bias=0.0, scale=scale), waits=bw)
                    bank_free[id(bank)] = t_e
                    scr = sqb_scr[(ec * 4 + tt) % 2]
                    sq_w = [t_e, t_on]
                    if toks_sq:
                        sq_w.append(toks_sq[-1][1])
                    t_s = p.comp("scalar", lambda e, tt=tt, buf=buf, scr=scr: e.activation(
                        scr[:], buf[:, tt * 512:(tt + 1) * 512], AF.Square),
                        waits=sq_w)
                    nm_w = [t_s]
                    if ec == 0 and toks_sq and toks_sq[0][0] is None:
                        nm_w += [t for (_n, t) in toks_sq if _n is None]
                    t_nm = p.comp("tensor", lambda e, tt=tt, ec=ec, scr=scr, psn_dst=psn_dst: e.matmul(
                        psn_dst[0:1, tt * 512:(tt + 1) * 512], ones_col[:],
                        scr[:], start=(ec == 0), stop=(ec == 7)),
                        waits=nm_w)
                    toks_sq.append((t_s, t_nm))
                t_st = p.dma("sync", lambda e, ec=ec, buf=buf, out_dram=out_dram: e.dma_start(
                    out=out_dram[ec * 128:(ec + 1) * 128, :], in_=buf[:]),
                    waits=[toks_sq[-1][0]])
                last_ec.append(t_st)
            return toks_sq, last_ec

        tq_sq, tq_st = proj_fm_stream("wq", q_out, 1.0 / 32.0, ps_n)
        tq_row = []
        for tt in range(4):
            tq_row.append(p.comp("vector", lambda e, tt=tt: e.tensor_copy(
                qsq_row[0:1, tt * 512:(tt + 1) * 512],
                ps_n[0:1, tt * 512:(tt + 1) * 512]), waits=[tq_sq[-1][1]]))
        t_qm = p.comp("vector", lambda e: e.reduce_max(
            qmax_sb[:], qsq_row[0:1, :], axis=AX.X), waits=tq_row)

        tk_sq, tk_st = proj_fm_stream("wk", k_out, 1.0, ps_n, after=tq_row)
        tkn = []
        for tt in range(4):
            tkn.append(p.comp("scalar", lambda e, tt=tt: e.activation(
                nrm_row[0:1, tt * 512:(tt + 1) * 512],
                ps_n[0:1, tt * 512:(tt + 1) * 512], AF.Sqrt),
                waits=[tk_sq[-1][1]]))

        tv = []
        for i in range(NCH):
            banks = [ps_mm[0], ps_mm[1]]
            mm_last = None
            for d in range(8):
                for h in range(2):
                    w0 = [t_w["wv"][d], t_fm_all[-1]]
                    if d == 0 and id(banks[h]) in bank_free:
                        w0.append(bank_free[id(banks[h])])
                    mm_last = p.comp("tensor", lambda e, i=i, d=d, h=h, banks=banks: e.matmul(
                        banks[h][:], xh_fm[d][:, i * 128:(i + 1) * 128],
                        w_sb["wv"][d][:, h * 512:(h + 1) * 512],
                        start=(d == 0), stop=(d == 7)),
                        waits=w0)
            for h in range(2):
                t_e = p.comp("vector", lambda e, i=i, h=h, banks=banks: e.tensor_copy(
                    v_tm[i][:, h * 512:(h + 1) * 512], banks[h][:]),
                    waits=[mm_last])
                bank_free[id(banks[h])] = t_e
                tv.append(t_e)

        st = []
        for i in range(NCH):
            st.append(p.dma("sync", lambda e, i=i: e.dma_start(
                out=v_out[i * 128:(i + 1) * 128, :], in_=v_tm[i][:]), waits=tv))
        st.append(p.dma("sync", lambda e: e.dma_start(out=kn_out, in_=nrm_row[:]),
                        waits=tkn))
        st.append(p.dma("sync", lambda e: e.dma_start(out=qm_out, in_=qmax_sb[:]),
                        waits=[t_qm]))
        p.raw("sync", lambda e: e.nop(), waits=p.barrier_tokens())
        p.emit()
    return nc




B_SHIFT = 20.0
B_MARG = 1.02


def build_launch_B():
    nc = new_nc()
    qg = nc.dram_tensor("qg", [D, B, S], BF16, kind="ExternalInput").ap()
    vg = nc.dram_tensor("vg", [B, S, D], BF16, kind="ExternalInput").ap()
    kf_in = nc.dram_tensor("kf", [D, TLOC], BF16, kind="ExternalInput").ap()
    kn_in = nc.dram_tensor("kn", [1, TLOC], F32, kind="ExternalInput").ap()
    qm_in = nc.dram_tensor("qm8", [N_CORES, 1], F32, kind="ExternalInput").ap()
    x_in = nc.dram_tensor("x", [TLOC, D], F32, kind="ExternalInput").ap()
    y_out = nc.dram_tensor("y", [TLOC, D], F32, kind="ExternalOutput").ap()

    SBLK = 2048
    NSB = S // SBLK
    NIB = SBLK // 128
    JC = 256
    NJC = SLOC // JC

    with ExitStack() as ctx, Prog(nc) as p:
        E = ctx.enter_context
        kf = [sb(E, nc, f"kfs{d}", [128, TLOC], BF16) for d in range(8)]
        q_s = [sb(E, nc, f"qs{d}", [128, SBLK], BF16) for d in range(8)]
        v_s = [sb(E, nc, f"vs{i}", [128, D], BF16) for i in range(NIB)]
        u_sb = [sb(E, nc, f"u{j}", [128, D], F32) for j in range(8)]
        l_sb = sb(E, nc, "l", [128, 8], F32)
        rl_sb = sb(E, nc, "rl", [128, 8], F32)
        pbuf = [sb(E, nc, f"pb{i}", [128, JC], BF16) for i in range(2)]
        kn_sb = sb(E, nc, "kns", [1, TLOC], F32)
        brow = sb(E, nc, "brow", [1, TLOC], BF16)
        qm_sb = sb(E, nc, "qms", [1, 16], F32)
        ones_col = sb(E, nc, "ones", [128, 1], BF16)
        ones_row = sb(E, nc, "onesr", [1, 128], BF16)
        x_sb = [sb(E, nc, f"x{i}", [128, D], F32) for i in range(2)]
        o_sb = [sb(E, nc, f"o{i}", [128, D], F32) for i in range(2)]

        ps_S = [ps(E, nc, f"psS{i}", [128, JC], F32) for i in range(2)]
        ps_acc = [ps(E, nc, f"psA{i}", [128, 512], F32) for i in range(4)]
        ps_l = ps(E, nc, "psl", [128, 2], F32)

        t_on = p.comp("gpsimd", lambda e: e.memset(ones_col[:], 1.0))
        t_or = p.comp("gpsimd", lambda e: e.memset(ones_row[:], 1.0))

        t_kf = [p.dma("sync", lambda e, d=d: e.dma_start(
            out=kf[d][:], in_=kf_in[d * 128:(d + 1) * 128, :])) for d in range(8)]
        t_kn = p.dma("sync", lambda e: e.dma_start(out=kn_sb[:], in_=kn_in))
        t_qm = p.dma("sync", lambda e: e.dma_start(
            out=qm_sb[0:1, 0:N_CORES], in_=qm_in.rearrange("a 1 -> 1 a")))

        t_b0 = p.comp("vector", lambda e: e.reduce_max(
            qm_sb[0:1, 15:16], qm_sb[0:1, 0:N_CORES], axis=AX.X), waits=[t_qm])
        t_b1 = p.comp("scalar", lambda e: e.activation(
            qm_sb[0:1, 14:15], qm_sb[0:1, 15:16], AF.Sqrt), waits=[t_b0])
        t_b2 = p.comp("vector", lambda e: e.tensor_scalar(
            qm_sb[0:1, 13:14], qm_sb[0:1, 14:15], -B_MARG, None, ALU.mult),
            waits=[t_b1])
        t_br = p.comp("vector", lambda e: e.tensor_scalar(
            brow[:], kn_sb[:], qm_sb[0:1, 13:14], B_SHIFT, ALU.mult, ALU.add),
            waits=[t_b2, t_kn])

        bank_tok = {}
        sb_consumed = []
        t_y = []
        last_o = {}
        last_xread = {}
        prev_ep = []
        for b in range(B):
            for sblk in range(NSB):
                lw = list(sb_consumed)
                sb_consumed = []
                t_q = [p.dma("sync", lambda e, d=d, b=b, sblk=sblk: e.dma_start(
                    out=q_s[d][:], in_=qg[d * 128:(d + 1) * 128, b,
                                          sblk * SBLK:(sblk + 1) * SBLK]),
                    waits=lw) for d in range(8)]
                t_v = [p.dma("scalar", lambda e, i=i, b=b, sblk=sblk: e.dma_start(
                    out=v_s[i][:], in_=vg[b, sblk * SBLK + i * 128:
                                          sblk * SBLK + (i + 1) * 128, :]),
                    waits=lw) for i in range(NIB)]
                for jc in range(NJC):
                    j0 = b * SLOC + jc * JC
                    accs = [ps_acc[k] for k in range(4)]
                    exp_toks = [None, None]
                    pv_last = [None, None]
                    for ib in range(NIB):
                        s_bank = ps_S[ib % 2]
                        pb = pbuf[ib % 2]
                        w0 = [t_q[7], t_kf[7], t_br, t_or]
                        if exp_toks[ib % 2]:
                            w0.append(exp_toks[ib % 2])
                        mm = None
                        for d in range(8):
                            mm = p.comp("tensor", lambda e, d=d, ib=ib, j0=j0, s_bank=s_bank: e.matmul(
                                s_bank[:], q_s[d][:, ib * 128:(ib + 1) * 128],
                                kf[d][:, j0:j0 + JC],
                                start=(d == 0), stop=False),
                                waits=w0 if d == 0 else ())
                        mm = p.comp("tensor", lambda e, j0=j0, s_bank=s_bank: e.matmul(
                            s_bank[:], ones_row[:], brow[0:1, j0:j0 + JC],
                            start=False, stop=True), waits=())
                        wexp = [mm]
                        if pv_last[ib % 2]:
                            wexp.append(pv_last[ib % 2])
                        t_exp = p.comp("scalar", lambda e, pb=pb, s_bank=s_bank: e.activation(
                            pb[:], s_bank[:], AF.Exp), waits=wexp)
                        exp_toks[ib % 2] = t_exp
                        lw0 = [t_exp, t_on]
                        if ib == 0 and id(ps_l) in bank_tok:
                            lw0.append(bank_tok[id(ps_l)])
                        lmm = None
                        for js in range(2):
                            lmm = p.comp("tensor", lambda e, js=js, pb=pb, ib=ib: e.matmul(
                                ps_l[:, js:js + 1], pb[:, js * 128:(js + 1) * 128],
                                ones_col[:], start=(ib == 0), stop=(ib == NIB - 1)),
                                waits=lw0 if js == 0 else ())
                        last = None
                        for js in range(2):
                            for eh in range(2):
                                acc = accs[js * 2 + eh]
                                w1 = [t_exp, t_v[ib]]
                                if ib == 0 and id(acc) in bank_tok:
                                    w1.append(bank_tok[id(acc)])
                                last = p.comp("tensor", lambda e, js=js, eh=eh, ib=ib, pb=pb, acc=acc: e.matmul(
                                    acc[:], pb[:, js * 128:(js + 1) * 128],
                                    v_s[ib][:, eh * 512:(eh + 1) * 512],
                                    start=(ib == 0), stop=(ib == NIB - 1)),
                                    waits=w1)
                        pv_last[ib % 2] = last
                    for js in range(2):
                        u = u_sb[jc * 2 + js]
                        for eh in range(2):
                            acc = accs[js * 2 + eh]
                            wv = [pv_last[(NIB - 1) % 2]] + (prev_ep if sblk == 0 else [])
                            if sblk == 0:
                                t_e = p.comp("vector", lambda e, u=u, eh=eh, acc=acc: e.tensor_copy(
                                    u[:, eh * 512:(eh + 1) * 512], acc[:]), waits=wv)
                            else:
                                t_e = p.comp("vector", lambda e, u=u, eh=eh, acc=acc: e.tensor_tensor(
                                    u[:, eh * 512:(eh + 1) * 512],
                                    u[:, eh * 512:(eh + 1) * 512], acc[:], ALU.add),
                                    waits=wv)
                            bank_tok[id(acc)] = t_e
                            sb_consumed.append(t_e)
                        lc = jc * 2 + js
                        wl = [lmm] + (prev_ep if sblk == 0 else [])
                        if sblk == 0:
                            t_l = p.comp("vector", lambda e, lc=lc, js=js: e.tensor_copy(
                                l_sb[:, lc:lc + 1], ps_l[:, js:js + 1]), waits=wl)
                        else:
                            t_l = p.comp("vector", lambda e, lc=lc, js=js: e.tensor_tensor(
                                l_sb[:, lc:lc + 1], l_sb[:, lc:lc + 1],
                                ps_l[:, js:js + 1], ALU.add), waits=wl)
                        bank_tok[id(ps_l)] = t_l
                        sb_consumed.append(t_l)
            ep_toks = []
            for j8 in range(8):
                ch = b * 8 + j8
                bi = j8 % 2
                t_rl = p.comp("vector", lambda e, j8=j8: e.reciprocal(
                    rl_sb[:, j8:j8 + 1], l_sb[:, j8:j8 + 1]), waits=sb_consumed)
                xb = x_sb[bi]
                ob = o_sb[bi]
                xw = [last_xread[bi]] if bi in last_xread else []
                t_x = p.dma("sync", lambda e, ch=ch, xb=xb: e.dma_start(
                    out=xb[:], in_=x_in[ch * 128:(ch + 1) * 128, :]), waits=xw)
                ow = [t_rl, t_x] + ([last_o[bi]] if bi in last_o else [])
                t_o = p.comp("vector", lambda e, j8=j8, xb=xb, ob=ob: e.scalar_tensor_tensor(
                    ob[:], u_sb[j8][:], rl_sb[:, j8:j8 + 1], xb[:],
                    ALU.mult, ALU.add), waits=ow)
                last_xread[bi] = t_o
                ep_toks.append(t_o)
                t_st = p.dma("sync", lambda e, ch=ch, ob=ob: e.dma_start(
                    out=y_out[ch * 128:(ch + 1) * 128, :], in_=ob[:]), waits=[t_o])
                last_o[bi] = t_st
                t_y.append(t_st)
            prev_ep = ep_toks
            sb_consumed = sb_consumed + ep_toks

        p.raw("sync", lambda e: e.nop(), waits=p.barrier_tokens())
        p.emit()
    return nc


def run_stage_B(outA, x_shards):
    nc = cached_nc("B", build_launch_B)
    qg = np.concatenate([o["q_fm"].reshape(D, B, SLOC) for o in outA], axis=2)
    vg = np.concatenate([o["v_tm"].reshape(B, SLOC, D) for o in outA], axis=1)
    qm8 = np.stack([o["qmaxsq"][0] for o in outA], axis=0)
    in_maps = [dict(qg=qg, vg=vg, kf=outA[c]["k_fm"], kn=outA[c]["knorm"],
                    qm8=qm8, x=x_shards[c]) for c in range(N_CORES)]
    res = run_bass_kernel_spmd(nc, in_maps, core_ids=list(range(N_CORES)))
    return [res.results[c]["y"] for c in range(N_CORES)]




def ln_chunk(p, i, xb, hb, stat, scr, t_xi, t_eps):
    c0 = (i % 16) * 8
    st_ = lambda j, c0=c0: stat[:, c0 + j:c0 + j + 1]
    t_s1 = p.comp("vector", lambda e, xb=xb, st_=st_: e.reduce_sum(
        st_(0), xb[:], axis=AX.X), waits=[t_xi])
    t_sq = p.comp("scalar", lambda e, xb=xb, scr=scr, st_=st_: e.activation(
        scr[:], xb[:], AF.Square, accum_out=st_(1)), waits=[t_xi])
    t_m = p.comp("vector", lambda e, st_=st_: e.tensor_scalar(
        st_(2), st_(0), 1.0 / D, None, ALU.mult), waits=[t_s1])
    t_v1 = p.comp("vector", lambda e, st_=st_: e.tensor_tensor(
        st_(3), st_(2), st_(2), ALU.mult), waits=[t_m])
    t_v2 = p.comp("vector", lambda e, st_=st_: e.tensor_scalar(
        st_(4), st_(1), 1.0 / D, st_(3), ALU.mult, ALU.subtract),
        waits=[t_sq, t_v1])
    t_sd = p.comp("scalar", lambda e, st_=st_: e.activation(
        st_(5), st_(4), AF.Sqrt, bias=EPS, scale=1.0), waits=[t_v2, t_eps])
    t_r = p.comp("vector", lambda e, st_=st_: e.reciprocal(
        st_(6), st_(5)), waits=[t_sd])
    return p.comp("vector", lambda e, xb=xb, hb=hb, st_=st_: e.tensor_scalar(
        hb[:], xb[:], st_(2), st_(6), ALU.subtract, ALU.mult),
        waits=[t_r, t_xi])


def build_launch_C1():
    nc = new_nc()
    x_in = nc.dram_tensor("x", [TLOC, D], F32, kind="ExternalInput").ap()
    rw_in = nc.dram_tensor("rw", [D, NEXP], F32, kind="ExternalInput").ap()
    xh_out = nc.dram_tensor("xh", [TLOC, D], BF16, kind="ExternalOutput").ap()
    lg_out = nc.dram_tensor("lg", [TLOC, NEXP], F32, kind="ExternalOutput").ap()

    with ExitStack() as ctx, Prog(nc) as p:
        E = ctx.enter_context
        x_sb = [sb(E, nc, f"x{i}", [128, D], F32) for i in range(2)]
        xh_tm32 = [sb(E, nc, f"xh{i}", [128, D], F32) for i in range(2)]
        xh_fm = [sb(E, nc, f"xf{d}", [128, TLOC], BF16) for d in range(8)]
        stat = sb(E, nc, "stat", [128, 8 * NCH], F32)
        sq_scr = [sb(E, nc, f"sqs{i}", [128, D], F32) for i in range(2)]
        rw32 = [sb(E, nc, f"rw{d}", [128, NEXP], F32) for d in range(8)]
        stage_f32 = [sb(E, nc, f"stg{i}", [128, 128], F32) for i in range(2)]
        lgt_sb = sb(E, nc, "lgt", [128, NEXP * NCH], F32)
        identf = sb(E, nc, "identf", [128, 128], F32)
        ident, t_id = build_identity(p, nc, E)
        cpool = ConstPool(p, nc, E)
        t_eps = cpool.reg(EPS)
        t_if0 = p.comp("gpsimd", lambda e: e.memset(identf[:], 1.0))
        t_if = p.comp("gpsimd", lambda e: e.affine_select(
            identf[:], identf[:], pattern=[[1, 128]], compare_op=ALU.is_equal,
            fill=0.0, base=0, channel_multiplier=-1), waits=[t_if0])

        ps_tpf = [ps(E, nc, f"ptf{i}", [128, 128], F32) for i in range(2)]
        ps_lgt = ps(E, nc, "plgt", [128, NEXP])

        t_rw = [p.dma("sync", lambda e, d=d: e.dma_start(
            out=rw32[d][:], in_=rw_in[d * 128:(d + 1) * 128, :]))
            for d in range(8)]

        t_fm_last, t_lg = emit_ln_fm_logits(
            p, nc, E, x_in, x_sb, xh_tm32, xh_fm, stat, sq_scr, ident, identf,
            t_id, t_if, t_eps, rw32, t_rw, None, ps_tpf, stage_f32, ps_lgt,
            lgt_sb, NEXP, store_xh=xh_out)
        for i in range(NCH):
            p.dma("sync", lambda e, i=i: e.dma_start(
                out=lg_out[i * 128:(i + 1) * 128, :],
                in_=lgt_sb[:, i * NEXP:(i + 1) * NEXP]), waits=[t_lg[i]])
        p.raw("sync", lambda e: e.nop(), waits=p.barrier_tokens())
        p.emit()
    return nc


NGCH = NTOK // 128
NST = CAPACITY // 512
DH = 4096
DUMP = CAPACITY + 127


def build_launch_C2():
    nc = new_nc()
    xh_in = nc.dram_tensor("xhf", [NTOK, D], BF16, kind="ExternalInput").ap()
    lg_in = nc.dram_tensor("lgf", [NTOK, NEXP], F32, kind="ExternalInput").ap()
    rb_in = nc.dram_tensor("rb", [1, NEXP], F32, kind="ExternalInput").ap()
    esel_in = nc.dram_tensor("esel", [1, NEXP], F32, kind="ExternalInput").ap()
    w1_in = nc.dram_tensor("w1", [D, DH], F32, kind="ExternalInput").ap()
    b1_in = nc.dram_tensor("b1c", [128, DH // 128], F32, kind="ExternalInput").ap()
    w2_in = nc.dram_tensor("w2", [DH, D], F32, kind="ExternalInput").ap()
    b2_in = nc.dram_tensor("b2c", [128, D // 128], F32, kind="ExternalInput").ap()

    oute = nc.dram_tensor("oute", [CAPACITY, D], BF16, kind="ExternalOutput").ap()
    base_out = nc.dram_tensor("base", [NGCH, NEXP], F32, kind="ExternalOutput").ap()

    xg_dram = nc.dram_tensor("xg", [CAPACITY + 128, D], BF16, kind="Internal").ap()
    base_dram = nc.dram_tensor("based", [NGCH, NEXP], F32, kind="Internal").ap()

    with ExitStack() as ctx, Prog(nc) as p:
        E = ctx.enter_context
        lg_sb = [sb(E, nc, f"lg{i}", [128, NEXP], F32) for i in range(3)]
        sel_bf = sb(E, nc, "selbf", [128, NGCH * NEXP], BF16)
        sel_f = sb(E, nc, "self", [128, NGCH * NEXP], F32)
        scr8 = sb(E, nc, "scr8", [128, 4 * 3 * NEXP], F32)
        rk_sb = sb(E, nc, "rks", [128, 3 * NEXP], F32)
        cw = sb(E, nc, "cw", [128, 3 * 8], F32)
        stat2 = sb(E, nc, "stat2", [128, 16], F32)
        rb_sb = sb(E, nc, "rbs", [1, NEXP], F32)
        rb_bc = sb(E, nc, "rbbc", [128, NEXP], F32)
        esel_sb = sb(E, nc, "esls", [1, NEXP], F32)
        esel_bc = sb(E, nc, "eslbc", [128, NEXP], F32)
        neg_tile = sb(E, nc, "negt", [128, NEXP], F32)
        zero8 = sb(E, nc, "zero8", [128, NEXP], F32)
        dump_t = sb(E, nc, "dumpt", [128, 1], F32)
        Lstrict = sb(E, nc, "lstr", [128, 128], BF16)
        ones_row = sb(E, nc, "onesr", [1, 128], F32)
        ones_colb = sb(E, nc, "onescb", [128, 1], BF16)
        tot_fm = sb(E, nc, "totfm", [NEXP, NGCH], BF16)
        tot_tm = sb(E, nc, "tottm", [128, NEXP], BF16)
        base_tm = sb(E, nc, "bastm", [128, NEXP], F32)
        base_row = sb(E, nc, "basrw", [1, NGCH * NEXP], F32)
        base_bc = sb(E, nc, "basbc", [128, NGCH * NEXP], F32)
        xrow = [sb(E, nc, f"xr{i}", [128, D], BF16) for i in range(3)]
        idxi = [sb(E, nc, f"ixi{i}", [128, 1], I32) for i in range(3)]
        w1_sb = [sb(E, nc, f"w1{d}", [128, DH], BF16) for d in range(8)]
        w2_sb = [sb(E, nc, f"w2{h}", [128, D], BF16) for h in range(32)]
        b1_sb = sb(E, nc, "b1s", [128, DH // 128], F32)
        b2_sb = sb(E, nc, "b2s", [128, D // 128], F32)
        xg_tm = [sb(E, nc, f"xgt{i}", [128, D], BF16) for i in range(4)]
        xg_fm = [sb(E, nc, f"xgf{d}", [128, 512], BF16) for d in range(8)]
        h_fm = [sb(E, nc, f"hf{h}", [128, 512], BF16) for h in range(32)]
        o_fm = [sb(E, nc, f"of{d}", [128, 512], BF16) for d in range(8)]
        o_tm = xg_tm

        ps_t = [ps(E, nc, f"pst{i}", [128, 128], BF16) for i in range(2)]
        ps_sp = ps(E, nc, "pssp", [128, NEXP])
        ps_tot = ps(E, nc, "pstot", [NEXP, NGCH])
        ps_m1 = ps(E, nc, "psm1", [128, 512])
        ps_m2 = ps(E, nc, "psm2", [128, 512])

        ident, t_id = build_identity(p, nc, E)
        cpool = ConstPool(p, nc, E)

        t_c0 = p.comp("gpsimd", lambda e: e.memset(neg_tile[:], -1e30))
        t_z8 = p.comp("gpsimd", lambda e: e.memset(zero8[:], 0.0))
        t_dp = p.comp("gpsimd", lambda e: e.memset(dump_t[:], 99999.0))
        t_or = p.comp("gpsimd", lambda e: e.memset(ones_row[:], 1.0))
        t_ocb = p.comp("gpsimd", lambda e: e.memset(ones_colb[:], 1.0))
        t_ls0 = p.comp("gpsimd", lambda e: e.memset(Lstrict[:], 1.0))
        t_ls = p.comp("gpsimd", lambda e: e.affine_select(
            Lstrict[:], Lstrict[:], pattern=[[1, 128]], compare_op=ALU.is_gt,
            fill=0.0, base=0, channel_multiplier=-1), waits=[t_ls0])

        t_rb = p.dma("sync", lambda e: e.dma_start(out=rb_sb[:], in_=rb_in))
        t_es = p.dma("sync", lambda e: e.dma_start(out=esel_sb[:], in_=esel_in))
        t_x = p.comp("tensor", lambda e: e.matmul(
            ps_sp[:], ones_row[:], rb_sb[:], start=True, stop=True),
            waits=[t_rb, t_or])
        t_rbb = p.comp("vector", lambda e: e.tensor_copy(rb_bc[:], ps_sp[:]),
                       waits=[t_x])
        t_x = p.comp("tensor", lambda e: e.matmul(
            ps_sp[:], ones_row[:], esel_sb[:], start=True, stop=True),
            waits=[t_es, t_rbb])
        t_esb = p.comp("vector", lambda e: e.tensor_copy(esel_bc[:], ps_sp[:]),
                       waits=[t_x])

        t_w1 = cast_load_weight(p, w1_in, w1_sb)
        t_w2 = cast_load_weight(p, w2_in, w2_sb)
        t_b1 = p.dma("sync", lambda e: e.dma_start(out=b1_sb[:], in_=b1_in))
        t_b2 = p.dma("sync", lambda e: e.dma_start(out=b2_sb[:], in_=b2_in))

        t_sel = [None] * NGCH
        t_tots = []
        for c in range(NGCH):
            lb = lg_sb[c % 3]
            lw = [t_sel[c - 3]] if c >= 3 else [t_rbb]
            t_lg = p.dma("sync", lambda e, c=c, lb=lb: e.dma_start(
                out=lb[:], in_=lg_in[c * 128:(c + 1) * 128, :]), waits=lw)
            k0 = (c % 4) * 3 * NEXP
            A = lambda k0=k0: scr8[:, k0:k0 + NEXP]
            Bb = lambda k0=k0: scr8[:, k0 + NEXP:k0 + 2 * NEXP]
            Cc = lambda k0=k0: scr8[:, k0 + 2 * NEXP:k0 + 3 * NEXP]
            s0 = (c % 8) * 2
            guard = [t_sel[c - 4]] if c >= 4 else [t_z8, t_c0, t_esb]
            t0 = p.comp("vector", lambda e, lb=lb: e.tensor_tensor(
                lb[:], lb[:], rb_bc[:], ALU.add), waits=[t_lg])
            t1 = p.comp("vector", lambda e, lb=lb, s0=s0: e.reduce_max(
                stat2[:, s0:s0 + 1], lb[:], axis=AX.X), waits=[t0])
            t2 = p.comp("vector", lambda e, lb=lb, Cc=Cc, s0=s0: e.tensor_scalar(
                Cc(), lb[:], stat2[:, s0:s0 + 1], None, ALU.is_equal),
                waits=[t1] + guard)
            t3 = p.comp("vector", lambda e, A=A, Cc=Cc: e.tensor_tensor_scan(
                A(), Cc(), zero8[:], 0.0, ALU.add, ALU.add), waits=[t2])
            t4a = p.comp("vector", lambda e, A=A: e.tensor_scalar(
                A(), A(), 1.0, None, ALU.is_equal), waits=[t3])
            t4 = p.comp("vector", lambda e, A=A, Cc=Cc: e.tensor_tensor(
                A(), A(), Cc(), ALU.mult), waits=[t4a])
            t6 = p.comp("vector", lambda e, lb=lb, A=A, Bb=Bb: e.scalar_tensor_tensor(
                Bb(), A(), -1e30, lb[:], ALU.mult, ALU.add), waits=[t4, t0])
            t7 = p.comp("vector", lambda e, Bb=Bb, s0=s0: e.reduce_max(
                stat2[:, s0 + 1:s0 + 2], Bb(), axis=AX.X), waits=[t6])
            t8 = p.comp("vector", lambda e, Bb=Bb, Cc=Cc, s0=s0: e.tensor_scalar(
                Cc(), Bb(), stat2[:, s0 + 1:s0 + 2], None, ALU.is_equal),
                waits=[t7])
            t9 = p.comp("vector", lambda e, Bb=Bb, Cc=Cc: e.tensor_tensor_scan(
                Bb(), Cc(), zero8[:], 0.0, ALU.add, ALU.add), waits=[t8])
            ta0 = p.comp("vector", lambda e, Bb=Bb: e.tensor_scalar(
                Bb(), Bb(), 1.0, None, ALU.is_equal), waits=[t9])
            ta = p.comp("vector", lambda e, Bb=Bb, Cc=Cc: e.tensor_tensor(
                Bb(), Bb(), Cc(), ALU.mult), waits=[ta0])
            tf = p.comp("vector", lambda e, c=c, A=A, Bb=Bb: e.tensor_tensor(
                sel_f[:, c * NEXP:(c + 1) * NEXP], A(), Bb(), ALU.add),
                waits=[ta])
            tb = p.comp("vector", lambda e, c=c: e.tensor_copy(
                sel_bf[:, c * NEXP:(c + 1) * NEXP],
                sel_f[:, c * NEXP:(c + 1) * NEXP]), waits=[tf])
            t_sel[c] = tb
            t_tots.append(p.comp("tensor", lambda e, c=c: e.matmul(
                ps_tot[0:NEXP, c:c + 1], sel_bf[:, c * NEXP:(c + 1) * NEXP],
                ones_colb[:], start=True, stop=True),
                waits=[tb, t_ocb]))

        t_te = p.comp("vector", lambda e: e.tensor_copy(
            tot_fm[:], ps_tot[0:NEXP, :]), waits=[t_tots[-1]])
        t_tt = p.comp("tensor", lambda e: e.transpose(
            ps_t[0][0:128, 0:NEXP], tot_fm[:], ident[0:NEXP, 0:NEXP]),
            waits=[t_te, t_id])
        t_tt2 = p.comp("vector", lambda e: e.tensor_copy(
            tot_tm[:], ps_t[0][0:128, 0:NEXP]), waits=[t_tt])
        t_bs = p.comp("tensor", lambda e: e.matmul(
            ps_sp[:], Lstrict[:], tot_tm[:], start=True, stop=True),
            waits=[t_tt2, t_ls, t_esb])
        t_be = p.comp("vector", lambda e: e.tensor_copy(base_tm[:], ps_sp[:]),
                      waits=[t_bs])
        t_bd = p.dma("sync", lambda e: e.dma_start(out=base_dram, in_=base_tm[:]),
                     waits=[t_be])
        t_bo = p.dma("scalar", lambda e: e.dma_start(out=base_out, in_=base_tm[:]),
                     waits=[t_be])
        t_br = p.dma("sync", lambda e: e.dma_start(
            out=base_row[:], in_=base_dram.rearrange("c e -> (c e)").unsqueeze(0)),
            waits=[t_bd])
        t_bb = []
        for hh in range(2):
            t_x = p.comp("tensor", lambda e, hh=hh: e.matmul(
                (ps_m1 if hh == 0 else ps_m2)[:], ones_row[:],
                base_row[0:1, hh * 512:(hh + 1) * 512], start=True, stop=True),
                waits=[t_br, t_or])
            t_bb.append(p.comp("vector", lambda e, hh=hh: e.tensor_copy(
                base_bc[:, hh * 512:(hh + 1) * 512],
                (ps_m1 if hh == 0 else ps_m2)[:]), waits=[t_x]))

        breg = {}

        def get_breg(e):
            if "r" not in breg:
                breg["r"] = e.to_reg(CAPACITY - 1)
            return breg["r"]

        t_scat = []
        t_sp_prev = None
        for c in range(NGCH):
            xb = xrow[c % 3]
            ib = idxi[c % 3]
            k0 = (c % 3) * NEXP
            w0 = (c % 3) * 8
            lw = [t_scat[c - 3]] if c >= 3 else []
            t_xr = p.dma("sync", lambda e, c=c, xb=xb: e.dma_start(
                out=xb[:], in_=xh_in[c * 128:(c + 1) * 128, :]), waits=lw)
            spw = [t_sel[c], t_ls, t_bb[-1]]
            if t_sp_prev:
                spw.append(t_sp_prev)
            t_sp = p.comp("tensor", lambda e, c=c: e.matmul(
                ps_sp[:], Lstrict[:], sel_bf[:, c * NEXP:(c + 1) * NEXP],
                start=True, stop=True), waits=spw)
            t_r8 = p.comp("vector", lambda e, c=c, k0=k0: e.tensor_tensor(
                rk_sb[:, k0:k0 + NEXP], ps_sp[:],
                base_bc[:, c * NEXP:(c + 1) * NEXP], ALU.add),
                waits=[t_sp] + ([t_scat[c - 3]] if c >= 3 else []))
            t_sp_prev = t_r8
            t_e1 = p.comp("vector", lambda e, k0=k0: e.tensor_tensor(
                rk_sb[:, k0:k0 + NEXP], rk_sb[:, k0:k0 + NEXP], esel_bc[:],
                ALU.mult), waits=[t_r8])
            t_rc = p.comp("vector", lambda e, k0=k0, w0=w0: e.reduce_sum(
                cw[:, w0:w0 + 1], rk_sb[:, k0:k0 + NEXP], axis=AX.X),
                waits=[t_e1])
            t_e2 = p.comp("vector", lambda e, c=c, k0=k0: e.tensor_tensor(
                rk_sb[:, k0:k0 + NEXP], sel_f[:, c * NEXP:(c + 1) * NEXP],
                esel_bc[:], ALU.mult), waits=[t_rc, t_sel[c]])
            t_sc = p.comp("vector", lambda e, k0=k0, w0=w0: e.reduce_sum(
                cw[:, w0 + 1:w0 + 2], rk_sb[:, k0:k0 + NEXP], axis=AX.X),
                waits=[t_e2])
            t_v1 = p.comp("vector", lambda e, w0=w0: e.tensor_scalar(
                cw[:, w0 + 2:w0 + 3], cw[:, w0:w0 + 1], float(CAPACITY), None,
                ALU.is_lt), waits=[t_rc, t_sc])
            t_v2 = p.comp("vector", lambda e, w0=w0: e.tensor_tensor(
                cw[:, w0 + 2:w0 + 3], cw[:, w0 + 2:w0 + 3],
                cw[:, w0 + 1:w0 + 2], ALU.mult), waits=[t_v1])
            t_ia = p.comp("vector", lambda e, w0=w0: e.tensor_scalar(
                cw[:, w0 + 3:w0 + 4], cw[:, w0:w0 + 1], 99999.0, None,
                ALU.subtract), waits=[t_v2, t_dp])
            t_ib = p.comp("vector", lambda e, w0=w0: e.tensor_tensor(
                cw[:, w0 + 3:w0 + 4], cw[:, w0 + 3:w0 + 4],
                cw[:, w0 + 2:w0 + 3], ALU.mult), waits=[t_ia])
            t_ix = p.comp("vector", lambda e, w0=w0: e.tensor_scalar(
                cw[:, w0 + 3:w0 + 4], cw[:, w0 + 3:w0 + 4], 99999.0, None,
                ALU.add), waits=[t_ib])
            t_ii = p.comp("vector", lambda e, ib=ib, w0=w0: e.tensor_copy(
                ib[:], cw[:, w0 + 3:w0 + 4]),
                waits=[t_ix] + ([t_scat[c - 3]] if c >= 3 else []))
            t_scat.append(p.dma("gpsimd", lambda e, xb=xb, ib=ib: e.indirect_dma_start(
                out=xg_dram[:],
                out_offset=IndirectOffsetOnAxis(ap=ib[:, 0:1], axis=0),
                in_=xb[:], in_offset=None, bounds_check=get_breg(e),
                oob_is_err=False), waits=[t_xr, t_ii]))

        bank = {0: ps_m1, 1: ps_m2}
        bank_tok = {}
        tr_tok = {}
        ps_t_users = [t_tt]
        t_out = []
        for st in range(NST):
            t_xg = []
            for i in range(4):
                lw = list(t_scat) if st == 0 and i == 0 else []
                lw += [tr_tok[(st - 1, i)]] if (st - 1, i) in tr_tok else []
                t_xg.append(p.dma("sync", lambda e, st=st, i=i: e.dma_start(
                    out=xg_tm[i][:],
                    in_=xg_dram[st * 512 + i * 128:st * 512 + (i + 1) * 128, :]),
                    waits=lw))
            tr_ev = []
            for i in range(4):
                for d in range(8):
                    pt = ps_t[(i * 8 + d) % 2]
                    k = i * 8 + d
                    if k >= 2:
                        pw = [tr_ev[k - 2]]
                    else:
                        pw = list(ps_t_users)
                    t_tr = p.comp("tensor", lambda e, i=i, d=d, pt=pt: e.transpose(
                        pt[:], xg_tm[i][:, d * 128:(d + 1) * 128], ident[:]),
                        waits=[t_xg[i], t_id] + pw)
                    t_ev = p.comp("vector", lambda e, i=i, d=d, pt=pt: e.tensor_copy(
                        xg_fm[d][:, i * 128:(i + 1) * 128], pt[:]), waits=[t_tr])
                    tr_ev.append(t_ev)
            t_trl = tr_ev[-1]
            t_hl = []
            for hc in range(32):
                bk = bank[hc % 2]
                w0 = [t_trl, t_w1[7]]
                if id(bk) in bank_tok:
                    w0.append(bank_tok[id(bk)])
                mm = None
                for d in range(8):
                    mm = p.comp("tensor", lambda e, hc=hc, d=d, bk=bk: e.matmul(
                        bk[:], w1_sb[d][:, hc * 128:(hc + 1) * 128], xg_fm[d][:],
                        start=(d == 0), stop=(d == 7)),
                        waits=w0 if d == 0 else ())
                t_h = p.comp("scalar", lambda e, hc=hc, bk=bk: e.activation(
                    h_fm[hc][:], bk[:], AF.Gelu, bias=b1_sb[:, hc:hc + 1],
                    scale=1.0), waits=[mm, t_b1] + ([t_hl[hc - 2]] if hc >= 2 else []))
                bank_tok[id(bk)] = t_h
                t_hl.append(t_h)
            t_ol = []
            for dc in range(8):
                bk = bank[dc % 2]
                w0 = [t_hl[-1], t_w2[31]]
                if id(bk) in bank_tok:
                    w0.append(bank_tok[id(bk)])
                mm = None
                for hc in range(32):
                    mm = p.comp("tensor", lambda e, dc=dc, hc=hc, bk=bk: e.matmul(
                        bk[:], w2_sb[hc][:, dc * 128:(dc + 1) * 128], h_fm[hc][:],
                        start=(hc == 0), stop=(hc == 31)),
                        waits=w0 if hc == 0 else ())
                t_o = p.comp("vector", lambda e, dc=dc, bk=bk: e.tensor_scalar(
                    o_fm[dc][:], bk[:], b2_sb[:, dc:dc + 1], None, ALU.add),
                    waits=[mm, t_b2] + ([t_ol[dc - 2]] if dc >= 2 else []))
                bank_tok[id(bk)] = t_o
                t_ol.append(t_o)
            to_ev = []
            for i in range(4):
                for d in range(8):
                    pt = ps_t[(i * 8 + d) % 2]
                    k = i * 8 + d
                    prev2 = to_ev[k - 2] if k >= 2 else tr_ev[-1]
                    t_tr = p.comp("tensor", lambda e, i=i, d=d, pt=pt: e.transpose(
                        pt[:], o_fm[d][:, i * 128:(i + 1) * 128], ident[:]),
                        waits=[t_ol[-1], prev2] + ([t_out[-4]] if len(t_out) >= 4 else []))
                    t_ev = p.comp("vector", lambda e, i=i, d=d, pt=pt: e.tensor_copy(
                        o_tm[i][:, d * 128:(d + 1) * 128], pt[:]), waits=[t_tr])
                    to_ev.append(t_ev)
                t_o_st = p.dma("scalar", lambda e, st=st, i=i: e.dma_start(
                    out=oute[st * 512 + i * 128:st * 512 + (i + 1) * 128, :],
                    in_=o_tm[i][:]), waits=[to_ev[-1]])
                t_out.append(t_o_st)
                tr_tok[(st, i)] = t_o_st
            ps_t_users = to_ev[-2:]

        p.raw("sync", lambda e: e.nop(), waits=p.barrier_tokens())
        p.emit()
    return nc



def build_launch_C3():
    nc = new_nc()
    x_in = nc.dram_tensor("x", [TLOC, D], F32, kind="ExternalInput").ap()
    lg_in = nc.dram_tensor("lgmy", [TLOC, NEXP], F32, kind="ExternalInput").ap()
    rb_in = nc.dram_tensor("rb", [1, NEXP], F32, kind="ExternalInput").ap()
    bm_in = nc.dram_tensor("bmy", [1, NCH * NEXP], F32, kind="ExternalInput").ap()
    oa_in = nc.dram_tensor("oall", [NEXP * CAPACITY + 1, D], BF16,
                           kind="ExternalInput").ap()
    y_out = nc.dram_tensor("y", [TLOC, D], F32, kind="ExternalOutput").ap()

    GDUMP = NEXP * CAPACITY

    with ExitStack() as ctx, Prog(nc) as p:
        E = ctx.enter_context
        lg_sb = [sb(E, nc, f"lg{i}", [128, NEXP], F32) for i in range(2)]
        x_sb = [sb(E, nc, f"x{i}", [128, D], F32) for i in range(2)]
        g1 = [sb(E, nc, f"g1{i}", [128, D], BF16) for i in range(2)]
        g2 = [sb(E, nc, f"g2{i}", [128, D], BF16) for i in range(2)]
        oA = [sb(E, nc, f"oa{i}", [128, D], F32) for i in range(2)]
        oB = [sb(E, nc, f"ob{i}", [128, D], F32) for i in range(2)]
        idx1 = [sb(E, nc, f"ix1{i}", [128, 1], I32) for i in range(2)]
        idx2 = [sb(E, nc, f"ix2{i}", [128, 1], I32) for i in range(2)]
        sel_bf = sb(E, nc, "selbf", [128, NCH * NEXP], BF16)
        scr = sb(E, nc, "scr", [128, 112], F32)
        cw = sb(E, nc, "cw", [128, 2 * 8], F32)
        stat2 = sb(E, nc, "stat2", [128, 2 * 4], F32)
        rb_sb = sb(E, nc, "rbs", [1, NEXP], F32)
        rb_bc = sb(E, nc, "rbbc", [128, NEXP], F32)
        bm_row = sb(E, nc, "bmrw", [1, NCH * NEXP], F32)
        bm_bc = sb(E, nc, "bmbc", [128, NCH * NEXP], F32)
        neg_tile = sb(E, nc, "negt", [128, NEXP], F32)
        zero8 = sb(E, nc, "zero8", [128, NEXP], F32)
        gdump = sb(E, nc, "gdmp", [128, NEXP], F32)
        e2560 = sb(E, nc, "e2560", [128, NEXP], F32)
        Lstrict = sb(E, nc, "lstr", [128, 128], BF16)
        ones_row = sb(E, nc, "onesr", [1, 128], F32)

        ps_sp = ps(E, nc, "pssp", [128, NEXP])
        ps_bc = [ps(E, nc, f"psbc{i}", [128, 512]) for i in range(2)]

        cpool = ConstPool(p, nc, E)

        t_c0 = p.comp("gpsimd", lambda e: e.memset(neg_tile[:], -1e30))
        t_z8 = p.comp("gpsimd", lambda e: e.memset(zero8[:], 0.0))
        t_gd = p.comp("gpsimd", lambda e: e.memset(gdump[:], float(GDUMP)))
        t_or = p.comp("gpsimd", lambda e: e.memset(ones_row[:], 1.0))
        t_e25 = p.comp("gpsimd", lambda e: e.iota(
            e2560[:], pattern=[[CAPACITY, NEXP]], base=0, channel_multiplier=0,
            allow_small_or_imprecise_dtypes=True))
        t_ls0 = p.comp("gpsimd", lambda e: e.memset(Lstrict[:], 1.0))
        t_ls = p.comp("gpsimd", lambda e: e.affine_select(
            Lstrict[:], Lstrict[:], pattern=[[1, 128]], compare_op=ALU.is_gt,
            fill=0.0, base=0, channel_multiplier=-1), waits=[t_ls0])

        t_rb = p.dma("sync", lambda e: e.dma_start(out=rb_sb[:], in_=rb_in))
        t_bm = p.dma("sync", lambda e: e.dma_start(out=bm_row[:], in_=bm_in))
        t_x0 = p.comp("tensor", lambda e: e.matmul(
            ps_sp[:], ones_row[:], rb_sb[:], start=True, stop=True),
            waits=[t_rb, t_or])
        t_rbb = p.comp("vector", lambda e: e.tensor_copy(rb_bc[:], ps_sp[:]),
                       waits=[t_x0])
        t_bb = []
        for hh in range(0, NCH * NEXP, 512):
            bk = ps_bc[(hh // 512) % 2]
            n = min(512, NCH * NEXP - hh)
            t_x1 = p.comp("tensor", lambda e, hh=hh, n=n, bk=bk: e.matmul(
                bk[0:128, 0:n], ones_row[:], bm_row[0:1, hh:hh + n],
                start=True, stop=True),
                waits=[t_bm, t_or] + ([t_bb[-2]] if len(t_bb) >= 2 else []))
            t_bb.append(p.comp("vector", lambda e, hh=hh, n=n, bk=bk: e.tensor_copy(
                bm_bc[:, hh:hh + n], bk[0:128, 0:n]), waits=[t_x1]))

        t_y = []
        t_gath = []
        t_sp_prev = None
        slot_tok = {}
        for c in range(NCH):
            sl = c % 2
            k0 = sl * 56
            M1 = scr[:, k0:k0 + 8]
            M2 = scr[:, k0 + 8:k0 + 16]
            W = scr[:, k0 + 16:k0 + 24]
            R = scr[:, k0 + 24:k0 + 32]
            V = scr[:, k0 + 32:k0 + 40]
            T = scr[:, k0 + 40:k0 + 48]
            T2 = scr[:, k0 + 48:k0 + 56]
            s0 = sl * 4
            st = lambda j, s0=s0: stat2[:, s0 + j:s0 + j + 1]
            c0 = sl * 8
            cc = lambda j, c0=c0: cw[:, c0 + j:c0 + j + 1]
            lb = lg_sb[sl]
            xb = x_sb[sl]
            guard = [slot_tok[sl]] if sl in slot_tok else [t_z8, t_c0, t_gd, t_e25]
            t_lg = p.dma("sync", lambda e, c=c, lb=lb: e.dma_start(
                out=lb[:], in_=lg_in[c * 128:(c + 1) * 128, :]), waits=guard)
            t_xr = p.dma("scalar", lambda e, c=c, xb=xb: e.dma_start(
                out=xb[:], in_=x_in[c * 128:(c + 1) * 128, :]), waits=guard)
            t0 = p.comp("vector", lambda e, lb=lb: e.tensor_tensor(
                lb[:], lb[:], rb_bc[:], ALU.add), waits=[t_lg, t_rbb] + guard)
            t1 = p.comp("vector", lambda e, lb=lb, st=st: e.reduce_max(
                st(0), lb[:], axis=AX.X), waits=[t0])
            t2 = p.comp("vector", lambda e, lb=lb, T2=T2, st=st: e.tensor_scalar(
                T2, lb[:], st(0), None, ALU.is_equal), waits=[t1])
            t3 = p.comp("vector", lambda e, M1=M1, T2=T2: e.tensor_tensor_scan(
                M1, T2, zero8[:], 0.0, ALU.add, ALU.add), waits=[t2])
            t4a = p.comp("vector", lambda e, M1=M1: e.tensor_scalar(
                M1, M1, 1.0, None, ALU.is_equal), waits=[t3])
            t4 = p.comp("vector", lambda e, M1=M1, T2=T2: e.tensor_tensor(
                M1, M1, T2, ALU.mult), waits=[t4a])
            t5 = p.comp("vector", lambda e, lb=lb, M1=M1, T=T: e.scalar_tensor_tensor(
                T, M1, -1e30, lb[:], ALU.mult, ALU.add), waits=[t4, t0])
            t6 = p.comp("vector", lambda e, T=T, st=st: e.reduce_max(
                st(1), T, axis=AX.X), waits=[t5])
            t7 = p.comp("vector", lambda e, T=T, T2=T2, st=st: e.tensor_scalar(
                T2, T, st(1), None, ALU.is_equal), waits=[t6])
            t8 = p.comp("vector", lambda e, M2=M2, T2=T2: e.tensor_tensor_scan(
                M2, T2, zero8[:], 0.0, ALU.add, ALU.add), waits=[t7])
            t9a = p.comp("vector", lambda e, M2=M2: e.tensor_scalar(
                M2, M2, 1.0, None, ALU.is_equal), waits=[t8])
            t9 = p.comp("vector", lambda e, M2=M2, T2=T2: e.tensor_tensor(
                M2, M2, T2, ALU.mult), waits=[t9a])
            tsel = p.comp("vector", lambda e, c=c, M1=M1, M2=M2: e.tensor_tensor(
                sel_bf[:, c * NEXP:(c + 1) * NEXP], M1, M2, ALU.add),
                waits=[t9, t4])
            tg1 = p.comp("vector", lambda e, st=st: e.tensor_scalar(
                st(2), st(0), -1.0, None, ALU.mult), waits=[t1])
            tg2 = p.comp("scalar", lambda e, lb=lb, W=W, st=st: e.activation(
                W, lb[:], AF.Exp, bias=st(2), scale=1.0), waits=[tg1, t5])
            tg3 = p.comp("vector", lambda e, W=W, st=st: e.reduce_sum(
                st(3), W, axis=AX.X), waits=[tg2])
            tg4 = p.comp("vector", lambda e, st=st: e.reciprocal(
                st(2), st(3)), waits=[tg3])
            tg5 = p.comp("vector", lambda e, W=W, st=st: e.tensor_scalar(
                W, W, st(2), None, ALU.mult), waits=[tg4])
            spw = [tsel, t_ls, t_bb[-1]]
            if t_sp_prev:
                spw.append(t_sp_prev)
            t_sp = p.comp("tensor", lambda e, c=c: e.matmul(
                ps_sp[:], Lstrict[:], sel_bf[:, c * NEXP:(c + 1) * NEXP],
                start=True, stop=True), waits=spw)
            t_r8 = p.comp("vector", lambda e, c=c, R=R: e.tensor_tensor(
                R, ps_sp[:], bm_bc[:, c * NEXP:(c + 1) * NEXP], ALU.add),
                waits=[t_sp])
            t_sp_prev = t_r8
            t_va = p.comp("vector", lambda e, R=R, V=V: e.tensor_scalar(
                V, R, float(CAPACITY), None, ALU.is_lt), waits=[t_r8])
            t_sf = p.comp("vector", lambda e, M1=M1, M2=M2, T=T: e.tensor_tensor(
                T, M1, M2, ALU.add), waits=[t9, t4, t5])
            t_vc = p.comp("vector", lambda e, V=V, T=T: e.tensor_tensor(
                V, V, T, ALU.mult), waits=[t_va, t_sf])
            t_w = p.comp("vector", lambda e, W=W, V=V: e.tensor_tensor(
                W, W, V, ALU.mult), waits=[tg5, t_vc])
            t_gs = p.comp("vector", lambda e, R=R: e.tensor_tensor(
                R, R, e2560[:], ALU.add), waits=[t_vc])
            t_ga = p.comp("vector", lambda e, R=R, T=T: e.tensor_scalar(
                T, R, float(GDUMP), None, ALU.subtract), waits=[t_gs, t_sf])
            t_gb = p.comp("vector", lambda e, T=T, V=V: e.tensor_tensor(
                T, T, V, ALU.mult), waits=[t_ga])
            t_gg = p.comp("vector", lambda e, R=R, T=T: e.tensor_scalar(
                R, T, float(GDUMP), None, ALU.add), waits=[t_gb])
            t_t1 = p.comp("vector", lambda e, M1=M1, W=W, T=T: e.tensor_tensor(
                T, M1, W, ALU.mult), waits=[t_w, t_vc, t_gg])
            t_w1 = p.comp("vector", lambda e, T=T, cc=cc: e.reduce_sum(
                cc(0), T, axis=AX.X), waits=[t_t1])
            t_t2 = p.comp("vector", lambda e, M1=M1, R=R, T=T: e.tensor_tensor(
                T, M1, R, ALU.mult), waits=[t_w1, t_gg])
            t_p1 = p.comp("vector", lambda e, T=T, cc=cc: e.reduce_sum(
                cc(1), T, axis=AX.X), waits=[t_t2])
            t_t3 = p.comp("vector", lambda e, M2=M2, W=W, T=T: e.tensor_tensor(
                T, M2, W, ALU.mult), waits=[t_p1])
            t_w2 = p.comp("vector", lambda e, T=T, cc=cc: e.reduce_sum(
                cc(2), T, axis=AX.X), waits=[t_t3])
            t_t4 = p.comp("vector", lambda e, M2=M2, R=R, T=T: e.tensor_tensor(
                T, M2, R, ALU.mult), waits=[t_w2])
            t_p2 = p.comp("vector", lambda e, T=T, cc=cc: e.reduce_sum(
                cc(3), T, axis=AX.X), waits=[t_t4])
            gw = [t_gath[2 * (c - 2)], t_gath[2 * (c - 2) + 1]] if c >= 2 else []
            t_i1 = p.comp("vector", lambda e, sl=sl, cc=cc: e.tensor_copy(
                idx1[sl][:], cc(1)), waits=[t_p1] + gw + guard)
            t_i2 = p.comp("vector", lambda e, sl=sl, cc=cc: e.tensor_copy(
                idx2[sl][:], cc(3)), waits=[t_p2] + gw + guard)
            tga = p.dma("gpsimd", lambda e, sl=sl: e.indirect_dma_start(
                out=g1[sl][:], out_offset=None, in_=oa_in[:],
                in_offset=IndirectOffsetOnAxis(ap=idx1[sl][:, 0:1], axis=0),
                oob_is_err=False), waits=[t_i1])
            tgb = p.dma("gpsimd", lambda e, sl=sl: e.indirect_dma_start(
                out=g2[sl][:], out_offset=None, in_=oa_in[:],
                in_offset=IndirectOffsetOnAxis(ap=idx2[sl][:, 0:1], axis=0),
                oob_is_err=False), waits=[t_i2])
            t_gath += [tga, tgb]
            t_oa = p.comp("vector", lambda e, sl=sl, xb=xb, cc=cc: e.scalar_tensor_tensor(
                oA[sl][:], g1[sl][:], cc(0), xb[:], ALU.mult, ALU.add),
                waits=[tga, t_xr, t_w1])
            t_ob = p.comp("vector", lambda e, sl=sl, cc=cc: e.scalar_tensor_tensor(
                oB[sl][:], g2[sl][:], cc(2), oA[sl][:], ALU.mult, ALU.add),
                waits=[tgb, t_oa, t_w2])
            t_st = p.dma("sync", lambda e, c=c, sl=sl: e.dma_start(
                out=y_out[c * 128:(c + 1) * 128, :], in_=oB[sl][:]),
                waits=[t_ob])
            t_y.append(t_st)
            slot_tok[sl] = t_st
        p.raw("sync", lambda e: e.nop(), waits=p.barrier_tokens())
        p.emit()
    return nc


_NC_CACHE = {}


def cached_nc(key, builder, *args):
    if key not in _NC_CACHE:
        _NC_CACHE[key] = builder(*args)
    return _NC_CACHE[key]


def shard_x(x):
    out = []
    for c in range(N_CORES):
        out.append(np.ascontiguousarray(
            x[:, c * SLOC:(c + 1) * SLOC, :].reshape(TLOC, D)))
    return out


def unshard_x(shards):
    return np.concatenate([sh.reshape(B, SLOC, D) for sh in shards], axis=1)


def run_stage_A(x_shards, wq, wk, wv):
    nc = cached_nc("A", build_launch_A)
    in_maps = [dict(x=x_shards[c], wq=wq, wk=wk, wv=wv) for c in range(N_CORES)]
    res = run_bass_kernel_spmd(nc, in_maps, core_ids=list(range(N_CORES)))
    return [res.results[c] for c in range(N_CORES)]


def emit_ln_and_fm(p, nc, E, x_in, x_sb, xh_tm, xh_fm, stat, sq_scr, ident,
                   t_id, t_eps, store_xh=None):
    t_fm = [[None] * 8 for _ in range(NCH)]
    t_ld, t_xh = [], []
    hb_readers = {}
    for i in range(NCH):
        xb = x_sb[i % 2]
        hb = xh_tm[i % 2]
        free_w = list(hb_readers.get(i % 2, []))
        t_xi = p.dma("sync", lambda e, i=i, xb=xb: e.dma_start(
            out=xb[:], in_=x_in[i * 128:(i + 1) * 128, :]), waits=free_w)
        t_ld.append(t_xi)
        t_h = ln_chunk(p, i, xb, hb, stat, sq_scr[i % 2], t_xi, t_eps)
        t_xh.append(t_h)
        rd = []
        if store_xh is not None:
            rd.append(p.dma("scalar", lambda e, i=i, hb=hb: e.dma_start(
                out=store_xh[i * 128:(i + 1) * 128, :], in_=hb[:]),
                waits=[t_h]))
        for d in range(8):
            k = i * 8 + d
            pt_idx = k % 2
            prev2 = None
            if k >= 2:
                j = k - 2
                prev2 = t_fm[j // 8][j % 8]
            t_tr = p.comp("tensor", lambda e, d=d, hb=hb, pt_idx=pt_idx: e.transpose(
                nc._lnfm_ps[pt_idx][:], hb[:, d * 128:(d + 1) * 128], ident[:]),
                waits=[t_h, t_id] + ([prev2] if prev2 else []))
            t_ev = p.comp("vector", lambda e, i=i, d=d, pt_idx=pt_idx: e.tensor_copy(
                xh_fm[d][:, i * 128:(i + 1) * 128], nc._lnfm_ps[pt_idx][:]),
                waits=[t_tr])
            t_fm[i][d] = t_ev
            rd.append(t_tr)
        hb_readers[i % 2] = rd
    return t_ld, t_xh, t_fm[NCH - 1][7]


def transpose_weight(p, nc, E, w_sb_src, dst_chunks, t_src, ident, t_id, ps_pair,
                     guards=()):
    toks = []
    ev_hist = []
    for i in range(len(w_sb_src)):
        for d in range(len(dst_chunks)):
            k = i * len(dst_chunks) + d
            pt = ps_pair[k % 2]
            pw = [ev_hist[k - 2]] if k >= 2 else list(guards)
            t_tr = p.comp("tensor", lambda e, i=i, d=d, pt=pt: e.transpose(
                pt[:], w_sb_src[i][:, d * 128:(d + 1) * 128], ident[:]),
                waits=[t_src[i], t_id] + pw)
            t_ev = p.comp("vector", lambda e, i=i, d=d, pt=pt: e.tensor_copy(
                dst_chunks[d][:, i * 128:(i + 1) * 128], pt[:]), waits=[t_tr])
            ev_hist.append(t_ev)
            toks.append(t_ev)
    return toks


def emit_ln_fm_logits(p, nc, E, x_in, x_sb, xh_tm32, xh_fm, stat, sq_scr,
                      ident, identf, t_id, t_if, t_eps, gwT32, t_gw, ps_tp,
                      ps_tpf, stage_f32, ps_lgt, lgt_sb, nexp, store_xh=None):
    t_fm = [[None] * 8 for _ in range(NCH)]
    t_lg = []
    hb_readers = {}
    stage_tok = {}
    lg_prev = None
    for i in range(NCH):
        xb = x_sb[i % 2]
        hb = xh_tm32[i % 2]
        free_w = list(hb_readers.get(i % 2, []))
        t_xi = p.dma("sync", lambda e, i=i, xb=xb: e.dma_start(
            out=xb[:], in_=x_in[i * 128:(i + 1) * 128, :]), waits=free_w)
        t_h = ln_chunk(p, i, xb, hb, stat, sq_scr[i % 2], t_xi, t_eps)
        rd = []
        if store_xh is not None:
            rd.append(p.dma("gpsimd", lambda e, i=i, hb=hb: e.dma_start(
                out=store_xh[i * 128:(i + 1) * 128, :], in_=hb[:]),
                waits=[t_h]))
        mm_lg = None
        for d in range(8):
            k = i * 8 + d
            ptf = ps_tpf[k % 2]
            stg = stage_f32[k % 2]
            pw = []
            if k >= 2:
                j = k - 2
                pw.append(t_fm[j // 8][j % 8])
            t_tr = p.comp("tensor", lambda e, d=d, hb=hb, ptf=ptf: e.transpose(
                ptf[:], hb[:, d * 128:(d + 1) * 128], identf[:]),
                waits=[t_h, t_if] + pw)
            sg = [stage_tok[k % 2]] if (k % 2) in stage_tok else []
            t_st = p.comp("vector", lambda e, ptf=ptf, stg=stg: e.tensor_copy(
                stg[:], ptf[:]), waits=[t_tr] + sg)
            t_ev = p.comp("vector", lambda e, i=i, d=d, ptf=ptf: e.tensor_copy(
                xh_fm[d][:, i * 128:(i + 1) * 128], ptf[:]), waits=[t_st])
            t_fm[i][d] = t_ev
            rd.append(t_tr)
            w0 = [t_st, t_gw[d]]
            if d == 0 and lg_prev:
                w0.append(lg_prev)
            mm_lg = p.comp("tensor", lambda e, d=d, stg=stg, nexp=nexp: e.matmul(
                ps_lgt[0:128, 0:nexp], stg[:], gwT32[d][0:128, 0:nexp],
                start=(d == 0), stop=(d == 7)), waits=w0)
            stage_tok[k % 2] = mm_lg
        t_le = p.comp("vector", lambda e, i=i, nexp=nexp: e.tensor_copy(
            lgt_sb[:, i * nexp:(i + 1) * nexp], ps_lgt[0:128, 0:nexp]),
            waits=[mm_lg])
        lg_prev = t_le
        t_lg.append(t_le)
        hb_readers[i % 2] = rd
    return t_fm[NCH - 1][7], t_lg


def build_launch_D():
    nc = new_nc()
    x_in = nc.dram_tensor("x", [TLOC, D], F32, kind="ExternalInput").ap()
    wv_in = nc.dram_tensor("wv", [D, D], F32, kind="ExternalInput").ap()
    bv_in = nc.dram_tensor("bv", [1, D], F32, kind="ExternalInput").ap()
    y_out = nc.dram_tensor("y", [TLOC, D], F32, kind="ExternalOutput").ap()

    with ExitStack() as ctx, Prog(nc) as p:
        E = ctx.enter_context
        x_sb = [sb(E, nc, f"x{i}", [128, D], F32) for i in range(2)]
        xh_tm = [sb(E, nc, f"xh{i}", [128, D], BF16) for i in range(2)]
        xh_fm = [sb(E, nc, f"xf{d}", [128, TLOC], BF16) for d in range(8)]
        stat = sb(E, nc, "stat", [128, 8 * NCH], F32)
        sq_scr = [sb(E, nc, f"sqs{i}", [128, D], F32) for i in range(2)]
        wv_sb = [sb(E, nc, f"wv{d}", [128, D], BF16) for d in range(8)]
        wvT = [sb(E, nc, f"wvt{d}", [128, D], BF16) for d in range(8)]
        bv_sb = sb(E, nc, "bvs", [1, D], F32)
        bv_bc = sb(E, nc, "bvbc", [128, D], F32)
        xr_sb = [sb(E, nc, f"xr{i}", [128, D], F32) for i in range(2)]
        o_sb = [sb(E, nc, f"o{i}", [128, D], F32) for i in range(2)]
        ones_row = sb(E, nc, "onesr", [1, 128], F32)
        ident, t_id = build_identity(p, nc, E)
        cpool = ConstPool(p, nc, E)
        t_eps = cpool.reg(EPS)

        ps_tp = [ps(E, nc, f"pst{i}", [128, 128], BF16) for i in range(2)]
        nc._lnfm_ps = ps_tp
        ps_mm = [ps(E, nc, f"psm{i}", [128, 512], F32) for i in range(2)]
        ps_bc = ps(E, nc, "psbc", [128, D], F32)

        t_or = p.comp("gpsimd", lambda e: e.memset(ones_row[:], 1.0))
        t_wv = cast_load_weight(p, wv_in, wv_sb)
        t_bv = p.dma("sync", lambda e: e.dma_start(out=bv_sb[:], in_=bv_in))
        t_bb = p.comp("tensor", lambda e: e.matmul(
            ps_bc[:, 0:512], ones_row[:], bv_sb[0:1, 0:512],
            start=True, stop=True), waits=[t_bv, t_or])
        t_bb2 = p.comp("tensor", lambda e: e.matmul(
            ps_bc[:, 512:1024], ones_row[:], bv_sb[0:1, 512:1024],
            start=True, stop=True), waits=[t_bb])
        t_bce = p.comp("vector", lambda e: e.tensor_copy(bv_bc[:], ps_bc[:]),
                       waits=[t_bb2])

        t_ld, t_xh, t_fm_last = emit_ln_and_fm(
            p, nc, E, x_in, x_sb, xh_tm, xh_fm, stat, sq_scr, ident, t_id, t_eps)
        t_wt = transpose_weight(p, nc, E, wv_sb, wvT, t_wv, ident, t_id, ps_tp,
                                guards=[t_fm_last])

        bank_tok = {}
        t_y = []
        slot_tok = {}
        for i in range(NCH):
            sl = i % 2
            xb2 = xr_sb[sl]
            ob = o_sb[sl]
            guard = [slot_tok[sl]] if sl in slot_tok else []
            t_xr = p.dma("sync", lambda e, i=i, xb2=xb2: e.dma_start(
                out=xb2[:], in_=x_in[i * 128:(i + 1) * 128, :]), waits=guard)
            t_xbv = p.comp("vector", lambda e, xb2=xb2: e.tensor_tensor(
                xb2[:], xb2[:], bv_bc[:], ALU.add), waits=[t_xr, t_bce] + guard)
            ev = []
            for h in range(2):
                bk = ps_mm[h]
                mm = None
                for d in range(8):
                    w0 = [t_wt[-1], t_fm_last]
                    if d == 0 and id(bk) in bank_tok:
                        w0.append(bank_tok[id(bk)])
                    mm = p.comp("tensor", lambda e, i=i, d=d, h=h, bk=bk: e.matmul(
                        bk[:], xh_fm[d][:, i * 128:(i + 1) * 128],
                        wvT[d][:, h * 512:(h + 1) * 512],
                        start=(d == 0), stop=(d == 7)), waits=w0)
                t_e = p.comp("vector", lambda e, h=h, bk=bk, ob=ob, xb2=xb2: e.tensor_tensor(
                    ob[:, h * 512:(h + 1) * 512], bk[:],
                    xb2[:, h * 512:(h + 1) * 512], ALU.add),
                    waits=[mm, t_xbv] + guard)
                bank_tok[id(bk)] = t_e
                ev.append(t_e)
            t_st = p.dma("sync", lambda e, i=i, ob=ob: e.dma_start(
                out=y_out[i * 128:(i + 1) * 128, :], in_=ob[:]), waits=ev)
            slot_tok[sl] = t_st
            t_y.append(t_st)
        p.raw("sync", lambda e: e.nop(), waits=p.barrier_tokens())
        p.emit()
    return nc


def build_launch_F():
    nc = new_nc()
    x_in = nc.dram_tensor("x", [TLOC, D], F32, kind="ExternalInput").ap()
    gw_in = nc.dram_tensor("gw", [NEXP, D], F32, kind="ExternalInput").ap()
    we_in = nc.dram_tensor("we", [NEXP * D, D], F32, kind="ExternalInput").ap()
    y_out = nc.dram_tensor("y", [TLOC, D], F32, kind="ExternalOutput").ap()

    with ExitStack() as ctx, Prog(nc) as p:
        E = ctx.enter_context
        x_sb = [sb(E, nc, f"x{i}", [128, D], F32) for i in range(2)]
        xh_tm32 = [sb(E, nc, f"xh{i}", [128, D], F32) for i in range(2)]
        xh_fm = [sb(E, nc, f"xf{d}", [128, TLOC], BF16) for d in range(8)]
        stat = sb(E, nc, "stat", [128, 8 * NCH], F32)
        sq_scr = [sb(E, nc, f"sqs{i}", [128, D], F32) for i in range(2)]
        gw_sb = sb(E, nc, "gws", [NEXP, D], F32)
        gwT32 = [sb(E, nc, f"gwt{d}", [128, NEXP], F32) for d in range(8)]
        stage_f32 = [sb(E, nc, f"stg{i}", [128, 128], F32) for i in range(2)]
        lgt_sb = sb(E, nc, "lgt", [128, NEXP * NCH], F32)
        wful = sb(E, nc, "wful", [128, NCH * NEXP], F32)
        scrm = sb(E, nc, "scrm", [128, 2 * 24], F32)
        stat2 = sb(E, nc, "stat2", [128, 2 * 6], F32)
        neg_tile = sb(E, nc, "negt", [128, NEXP], F32)
        zero8 = sb(E, nc, "zero8", [128, NEXP], F32)
        identf = sb(E, nc, "identf", [128, 128], F32)
        y_sb = [sb(E, nc, f"y{i}", [128, D], F32) for i in range(NCH)]
        we_sb = [[sb(E, nc, f"we{j}_{d}", [128, D], BF16) for d in range(8)]
                 for j in range(2)]
        ident, t_id = build_identity(p, nc, E)
        cpool = ConstPool(p, nc, E)
        t_eps = cpool.reg(EPS)

        ps_tpf = [ps(E, nc, f"ptf{i}", [128, 128], F32) for i in range(2)]
        ps_lgt = ps(E, nc, "plgt", [128, NEXP])
        ps_mm = [ps(E, nc, f"psm{i}", [128, 512], F32) for i in range(2)]

        t_c0 = p.comp("gpsimd", lambda e: e.memset(neg_tile[:], -1e30))
        t_z8 = p.comp("gpsimd", lambda e: e.memset(zero8[:], 0.0))
        t_if0 = p.comp("gpsimd", lambda e: e.memset(identf[:], 1.0))
        t_if = p.comp("gpsimd", lambda e: e.affine_select(
            identf[:], identf[:], pattern=[[1, 128]], compare_op=ALU.is_equal,
            fill=0.0, base=0, channel_multiplier=-1), waits=[t_if0])

        t_gw = p.dma("sync", lambda e: e.dma_start(out=gw_sb[:], in_=gw_in))
        t_gt = []
        for d in range(8):
            ptf = ps_tpf[d % 2]
            pw = [t_gt[d - 2]] if d >= 2 else []
            t_tr = p.comp("tensor", lambda e, d=d, ptf=ptf: e.transpose(
                ptf[0:128, 0:NEXP], gw_sb[:, d * 128:(d + 1) * 128],
                identf[0:NEXP, 0:NEXP]), waits=[t_gw, t_if] + pw)
            t_ev = p.comp("vector", lambda e, d=d, ptf=ptf: e.tensor_copy(
                gwT32[d][:], ptf[0:128, 0:NEXP]), waits=[t_tr])
            t_gt.append(t_ev)

        t_fm_last, t_lg = emit_ln_fm_logits(
            p, nc, E, x_in, x_sb, xh_tm32, xh_fm, stat, sq_scr, ident, identf,
            t_id, t_if, t_eps, gwT32, [t_gt[d] for d in range(8)], None,
            ps_tpf, stage_f32, ps_lgt, lgt_sb, NEXP)

        t_wf = []
        for i in range(NCH):
            lb = lgt_sb[:, i * NEXP:(i + 1) * NEXP]
            k0 = (i % 2) * 24
            M1 = scrm[:, k0:k0 + 8]
            M2 = scrm[:, k0 + 8:k0 + 16]
            T = scrm[:, k0 + 16:k0 + 24]
            s0 = (i % 2) * 6
            st = lambda j, s0=s0: stat2[:, s0 + j:s0 + j + 1]
            guard = [t_wf[i - 2]] if i >= 2 else [t_z8, t_c0]
            t1 = p.comp("vector", lambda e, lb=lb, st=st: e.reduce_max(
                st(0), lb, axis=AX.X), waits=[t_lg[i]] + guard)
            t2 = p.comp("vector", lambda e, lb=lb, T=T, st=st: e.tensor_scalar(
                T, lb, st(0), None, ALU.is_equal), waits=[t1])
            t3 = p.comp("vector", lambda e, M1=M1, T=T: e.tensor_tensor_scan(
                M1, T, zero8[:], 0.0, ALU.add, ALU.add), waits=[t2])
            t4a = p.comp("vector", lambda e, M1=M1: e.tensor_scalar(
                M1, M1, 1.0, None, ALU.is_equal), waits=[t3])
            t4 = p.comp("vector", lambda e, M1=M1, T=T: e.tensor_tensor(
                M1, M1, T, ALU.mult), waits=[t4a])
            t5 = p.comp("vector", lambda e, lb=lb, M1=M1, T=T: e.scalar_tensor_tensor(
                T, M1, -1e30, lb, ALU.mult, ALU.add), waits=[t4])
            t6 = p.comp("vector", lambda e, T=T, st=st: e.reduce_max(
                st(1), T, axis=AX.X), waits=[t5])
            t7 = p.comp("vector", lambda e, T=T, st=st: e.tensor_scalar(
                T, T, st(1), None, ALU.is_equal), waits=[t6])
            t8 = p.comp("vector", lambda e, M2=M2, T=T: e.tensor_tensor_scan(
                M2, T, zero8[:], 0.0, ALU.add, ALU.add), waits=[t7])
            t9a = p.comp("vector", lambda e, M2=M2: e.tensor_scalar(
                M2, M2, 1.0, None, ALU.is_equal), waits=[t8])
            t9 = p.comp("vector", lambda e, M2=M2, T=T: e.tensor_tensor(
                M2, M2, T, ALU.mult), waits=[t9a])
            ta = p.comp("vector", lambda e, st=st: e.tensor_tensor(
                st(2), st(1), st(0), ALU.subtract), waits=[t9, t6, t1])
            tb = p.comp("scalar", lambda e, st=st: e.activation(
                st(3), st(2), AF.Exp), waits=[ta])
            tc = p.comp("vector", lambda e, st=st: e.tensor_scalar(
                st(4), st(3), 1.0, None, ALU.add), waits=[tb])
            td = p.comp("vector", lambda e, st=st: e.reciprocal(
                st(5), st(4)), waits=[tc])
            te = p.comp("vector", lambda e, st=st: e.tensor_tensor(
                st(3), st(3), st(5), ALU.mult), waits=[td])
            tf = p.comp("vector", lambda e, M1=M1, st=st: e.tensor_scalar(
                M1, M1, st(5), None, ALU.mult), waits=[te])
            tg = p.comp("vector", lambda e, i=i, M1=M1, M2=M2, st=st: e.scalar_tensor_tensor(
                wful[:, i * NEXP:(i + 1) * NEXP], M2, st(3), M1,
                ALU.mult, ALU.add), waits=[tf])
            t_wf.append(tg)

        t_yi = []
        for i in range(NCH):
            t_yi.append(p.dma("scalar", lambda e, i=i: e.dma_start(
                out=y_sb[i][:], in_=x_in[i * 128:(i + 1) * 128, :])))

        bank_tok = {}
        t_we = [None, None]
        t_acc = {}
        for e0 in range(NEXP):
            j = e0 % 2
            wait_w = [t_acc[(e0 - 2, i)] for i in range(NCH)] if e0 >= 2 else []
            t_we[j] = cast_load_weight(
                p, we_in[e0 * D:(e0 + 1) * D, :], we_sb[j], waits=wait_w)
            for i in range(NCH):
                for h in range(2):
                    bk = ps_mm[h]
                    mm = None
                    for d in range(8):
                        w0 = [t_we[j][d], t_fm_last]
                        if d == 0 and id(bk) in bank_tok:
                            w0.append(bank_tok[id(bk)])
                        mm = p.comp("tensor", lambda e, i=i, d=d, h=h, bk=bk, j=j: e.matmul(
                            bk[:], xh_fm[d][:, i * 128:(i + 1) * 128],
                            we_sb[j][d][:, h * 512:(h + 1) * 512],
                            start=(d == 0), stop=(d == 7)), waits=w0)
                    t_e = p.comp("vector", lambda e, i=i, h=h, bk=bk, e0=e0: e.scalar_tensor_tensor(
                        y_sb[i][:, h * 512:(h + 1) * 512], bk[:],
                        wful[:, i * NEXP + e0:i * NEXP + e0 + 1],
                        y_sb[i][:, h * 512:(h + 1) * 512], ALU.mult, ALU.add),
                        waits=[mm, t_wf[i], t_yi[i]] +
                              ([t_acc[(e0 - 1, i)]] if e0 >= 1 else []))
                    bank_tok[id(bk)] = t_e
                t_acc[(e0, i)] = t_e
        for i in range(NCH):
            p.dma("sync", lambda e, i=i: e.dma_start(
                out=y_out[i * 128:(i + 1) * 128, :], in_=y_sb[i][:]),
                waits=[t_acc[(NEXP - 1, i)]])
        p.raw("sync", lambda e: e.nop(), waits=p.barrier_tokens())
        p.emit()
    return nc


NHEAD_A = 16
KA = 64


def build_launch_E1():
    nc = new_nc()
    x_in = nc.dram_tensor("x", [TLOC, D], F32, kind="ExternalInput").ap()
    wp_in = nc.dram_tensor("wp", [D, D], F32, kind="ExternalInput").ap()
    pm_in = nc.dram_tensor("pm", [NHEAD_A * KA, KA], F32, kind="ExternalInput").ap()
    pfm_out = nc.dram_tensor("pfm", [D, TLOC], BF16, kind="ExternalOutput").ap()
    ptm_out = nc.dram_tensor("ptm", [TLOC, D], BF16, kind="ExternalOutput").ap()
    qm_out = nc.dram_tensor("qmf", [D, TLOC], BF16, kind="ExternalOutput").ap()
    tc_out = nc.dram_tensor("tc", [B * 128, 8 * KA], F32, kind="ExternalOutput").ap()

    with ExitStack() as ctx, Prog(nc) as p:
        E = ctx.enter_context
        x_sb = [sb(E, nc, f"x{i}", [128, D], F32) for i in range(2)]
        xh_tm = [sb(E, nc, f"xh{i}", [128, D], BF16) for i in range(2)]
        xh_fm = [sb(E, nc, f"xf{d}", [128, TLOC], BF16) for d in range(8)]
        stat = sb(E, nc, "stat", [128, 8 * NCH], F32)
        sq_scr = [sb(E, nc, f"sqs{i}", [128, D], F32) for i in range(2)]
        wp_sb = [sb(E, nc, f"wp{d}", [128, D], BF16) for d in range(8)]
        wpT = [sb(E, nc, f"wpt{d}", [128, D], BF16) for d in range(8)]
        p_fm = [sb(E, nc, f"pf{d}", [128, TLOC], BF16) for d in range(8)]
        p_tm = [sb(E, nc, f"pt{i}", [128, D], BF16) for i in range(NCH)]
        pm_sb = sb(E, nc, "pms", [128, KA * 8], BF16)
        pmT = sb(E, nc, "pmt", [128, KA * 8], BF16)
        met_sb = sb(E, nc, "met", [128, KA * 8], BF16)
        qm_fm = [sb(E, nc, f"qf{d}", [128, TLOC], BF16) for d in range(8)]
        tc_sb = [sb(E, nc, f"tcs{b}", [128, 8 * KA], F32) for b in range(B)]
        ident, t_id = build_identity(p, nc, E)
        cpool = ConstPool(p, nc, E)
        t_eps = cpool.reg(EPS)

        ps_tp = [ps(E, nc, f"pst{i}", [128, 128], BF16) for i in range(2)]
        nc._lnfm_ps = ps_tp
        ps_mm = [ps(E, nc, f"psm{i}", [128, 512], F32) for i in range(2)]
        ps_sml = [ps(E, nc, f"pss{i}", [128, KA], F32) for i in range(2)]

        t_wp = cast_load_weight(p, wp_in, wp_sb)
        t_pm = [p.dma("gpsimd", lambda e, h=h: e.dma_start(
            out=pm_sb[(h % 2) * 64:(h % 2 + 1) * 64,
                      (h // 2) * KA:(h // 2 + 1) * KA],
            in_=pm_in[h * KA:(h + 1) * KA, :])) for h in range(NHEAD_A)]

        t_ld, t_xh, t_fm_last = emit_ln_and_fm(
            p, nc, E, x_in, x_sb, xh_tm, xh_fm, stat, sq_scr, ident, t_id, t_eps)
        t_wt = transpose_weight(p, nc, E, wp_sb, wpT, t_wp, ident, t_id, ps_tp,
                                guards=[t_fm_last])

        bank_tok = {}
        t_pf = []
        for ec in range(8):
            for tt in range(4):
                bk = ps_mm[tt % 2]
                mm = None
                for d in range(8):
                    w0 = [t_wt[-1], t_fm_last]
                    if d == 0 and id(bk) in bank_tok:
                        w0.append(bank_tok[id(bk)])
                    mm = p.comp("tensor", lambda e, ec=ec, tt=tt, d=d, bk=bk: e.matmul(
                        bk[:], wpT[d][:, ec * 128:(ec + 1) * 128],
                        xh_fm[d][:, tt * 512:(tt + 1) * 512],
                        start=(d == 0), stop=(d == 7)), waits=w0)
                t_e = p.comp("vector", lambda e, ec=ec, tt=tt, bk=bk: e.tensor_copy(
                    p_fm[ec][:, tt * 512:(tt + 1) * 512], bk[:]), waits=[mm])
                bank_tok[id(bk)] = t_e
                t_pf.append(t_e)
        for ec in range(8):
            p.dma("scalar", lambda e, ec=ec: e.dma_start(
                out=pfm_out[ec * 128:(ec + 1) * 128, :], in_=p_fm[ec][:]),
                waits=[t_pf[-1]])
        t_pt = []
        for i in range(NCH):
            for h in range(2):
                bk = ps_mm[h]
                mm = None
                for d in range(8):
                    w0 = [t_wt[-1], t_fm_last]
                    if d == 0 and id(bk) in bank_tok:
                        w0.append(bank_tok[id(bk)])
                    mm = p.comp("tensor", lambda e, i=i, d=d, h=h, bk=bk: e.matmul(
                        bk[:], xh_fm[d][:, i * 128:(i + 1) * 128],
                        wpT[d][:, h * 512:(h + 1) * 512],
                        start=(d == 0), stop=(d == 7)), waits=w0)
                t_e = p.comp("vector", lambda e, i=i, h=h, bk=bk: e.tensor_copy(
                    p_tm[i][:, h * 512:(h + 1) * 512], bk[:]), waits=[mm])
                bank_tok[id(bk)] = t_e
                t_pt.append(t_e)
            p.dma("scalar", lambda e, i=i: e.dma_start(
                out=ptm_out[i * 128:(i + 1) * 128, :], in_=p_tm[i][:]),
                waits=[t_pt[-1]])
        t_met = []
        for h in range(NHEAD_A):
            r0 = (h % 2) * 64
            cb = (h // 2) * KA
            pt_b = ps_tp[h % 2]
            pw = [t_met[h - 2]] if h >= 2 else [t_pf[-1]]
            t_tr = p.comp("tensor", lambda e, h=h, r0=r0, cb=cb, pt_b=pt_b: e.transpose(
                pt_b[0:64, 0:64], pm_sb[r0:r0 + 64, cb:cb + KA],
                ident[r0:r0 + 64, r0:r0 + 64]),
                waits=[t_pm[h], t_id] + pw)
            t_ev = p.comp("vector", lambda e, h=h, r0=r0, cb=cb, pt_b=pt_b: e.tensor_copy(
                pmT[r0:r0 + 64, cb:cb + KA], pt_b[0:64, 0:64]), waits=[t_tr])
            bk = ps_sml[h % 2]
            t_mm = p.comp("tensor", lambda e, h=h, r0=r0, cb=cb, bk=bk: e.matmul(
                bk[0:64, 0:KA], pmT[r0:r0 + 64, cb:cb + KA],
                pmT[r0:r0 + 64, cb:cb + KA], start=True, stop=True),
                waits=[t_ev])
            t_me = p.comp("scalar", lambda e, h=h, r0=r0, cb=cb, bk=bk: e.activation(
                met_sb[r0:r0 + 64, cb:cb + KA], bk[0:64, 0:KA], AF.Copy,
                bias=0.0, scale=0.125), waits=[t_mm])
            t_met.append(t_me)
        t_qm = []
        for h in range(NHEAD_A):
            r0 = (h % 2) * 64
            cb = (h // 2) * KA
            for tt in range(4):
                bk = ps_mm[tt % 2]
                w0 = [t_met[h], t_pf[-1]]
                if id(bk) in bank_tok:
                    w0.append(bank_tok[id(bk)])
                mm = p.comp("tensor", lambda e, h=h, r0=r0, cb=cb, tt=tt, bk=bk: e.matmul(
                    bk[0:64, :], met_sb[r0:r0 + 64, cb:cb + KA],
                    p_fm[h // 2][r0:r0 + 64, tt * 512:(tt + 1) * 512],
                    start=True, stop=True), waits=w0)
                t_e = p.comp("vector", lambda e, h=h, r0=r0, tt=tt, bk=bk: e.tensor_copy(
                    qm_sbuf_dummy[:] if False else
                    qm_fm[h // 2][r0:r0 + 64, tt * 512:(tt + 1) * 512],
                    bk[0:64, :]), waits=[mm])
                bank_tok[id(bk)] = t_e
                t_qm.append(t_e)
        for ec in range(8):
            p.dma("sync", lambda e, ec=ec: e.dma_start(
                out=qm_out[ec * 128:(ec + 1) * 128, :], in_=qm_fm[ec][:]),
                waits=[t_qm[-1]])
        t_tc = []
        for b in range(B):
            for h in range(NHEAD_A):
                r0 = (h % 2) * 64
                cb = (h // 2) * KA
                bk = ps_sml[h % 2]
                mm = None
                for ch in range(8):
                    w0 = [t_pt[-1]]
                    if ch == 0 and len(t_tc) >= 2:
                        w0.append(t_tc[-2])
                    mm = p.comp("tensor", lambda e, b=b, h=h, ch=ch, bk=bk, r0=r0: e.matmul(
                        bk[r0:r0 + 64, 0:KA],
                        p_tm[b * 8 + ch][:, h * KA:(h + 1) * KA],
                        p_tm[b * 8 + ch][:, h * KA:(h + 1) * KA],
                        start=(ch == 0), stop=(ch == 7),
                        tile_position=(0, r0)), waits=w0)
                t_e = p.comp("vector", lambda e, b=b, bk=bk, r0=r0, cb=cb: e.tensor_copy(
                    tc_sb[b][r0:r0 + 64, cb:cb + KA], bk[r0:r0 + 64, 0:KA]),
                    waits=[mm])
                t_tc.append(t_e)
        for b in range(B):
            p.dma("sync", lambda e, b=b: e.dma_start(
                out=tc_out[b * 128:(b + 1) * 128, :], in_=tc_sb[b][:]),
                waits=t_tc[(b + 1) * NHEAD_A - 2:(b + 1) * NHEAD_A])
        p.raw("sync", lambda e: e.nop(), waits=p.barrier_tokens())
        p.emit()
    return nc


def build_launch_E2():
    nc = new_nc()
    x_in = nc.dram_tensor("x", [TLOC, D], F32, kind="ExternalInput").ap()
    pfm_in = nc.dram_tensor("pfm", [D, TLOC], BF16, kind="ExternalInput").ap()
    ptm_in = nc.dram_tensor("ptm", [TLOC, D], BF16, kind="ExternalInput").ap()
    qm_in = nc.dram_tensor("qmf", [D, TLOC], BF16, kind="ExternalInput").ap()
    tpk_in = nc.dram_tensor("tpk", [N_CORES * B * 128, 8 * KA], F32,
                            kind="ExternalInput").ap()
    msk_in = nc.dram_tensor("mcmask", [1, N_CORES], F32, kind="ExternalInput").ap()
    trf_in = nc.dram_tensor("trf", [NHEAD_A * KA, KA], F32, kind="ExternalInput").ap()
    wm_in = nc.dram_tensor("wm", [D, D], F32, kind="ExternalInput").ap()
    y_out = nc.dram_tensor("y", [TLOC, D], F32, kind="ExternalOutput").ap()

    with ExitStack() as ctx, Prog(nc) as p:
        E = ctx.enter_context
        p_fm = [sb(E, nc, f"pf{d}", [128, TLOC], BF16) for d in range(8)]
        qm_fm = [sb(E, nc, f"qf{d}", [128, TLOC], BF16) for d in range(8)]
        p_tm = [sb(E, nc, f"pt{i}", [128, D], BF16) for i in range(NCH)]
        z_fm = [sb(E, nc, f"zf{d}", [128, TLOC], BF16) for d in range(8)]
        wm_sb = [sb(E, nc, f"wm{d}", [128, D], BF16) for d in range(8)]
        wmT = [sb(E, nc, f"wmt{d}", [128, D], BF16) for d in range(8)]
        trf_sb = sb(E, nc, "trfs", [128, KA * 8], BF16)
        S32 = [sb(E, nc, f"s32_{b}", [128, 8 * KA], F32) for b in range(B)]
        Sbf = [sb(E, nc, f"sbf_{b}", [128, 8 * KA], BF16) for b in range(B)]
        tld = [sb(E, nc, f"tld{i}", [128, 8 * KA], F32) for i in range(2)]
        mrow = sb(E, nc, "mrow", [1, N_CORES], F32)
        m_bc = sb(E, nc, "mbc", [128, N_CORES], F32)
        maskU = sb(E, nc, "masku", [128, 128], F32)
        spm = sb(E, nc, "spm", [128, 128], BF16)
        nud = sb(E, nc, "nud", [128, 128], BF16)
        x_sb = [sb(E, nc, f"x{i}", [128, D], F32) for i in range(2)]
        o_sb = [sb(E, nc, f"o{i}", [128, D], F32) for i in range(2)]
        ones_row = sb(E, nc, "onesr", [1, 128], F32)
        ident, t_id = build_identity(p, nc, E)
        cpool = ConstPool(p, nc, E)

        ps_tp = [ps(E, nc, f"pst{i}", [128, 128], BF16) for i in range(2)]
        ps_S = [ps(E, nc, f"psS{i}", [128, 128], F32) for i in range(2)]
        ps_y = [ps(E, nc, f"psy{i}", [128, 128], F32) for i in range(2)]
        ps_mm = [ps(E, nc, f"psm{i}", [128, 512], F32) for i in range(2)]

        t_or = p.comp("gpsimd", lambda e: e.memset(ones_row[:], 1.0))
        t_mk0 = p.comp("gpsimd", lambda e: e.memset(maskU[:], 1.0))
        t_mk = p.comp("gpsimd", lambda e: e.affine_select(
            maskU[:], maskU[:], pattern=[[1, 128]], compare_op=ALU.is_ge,
            fill=0.0, base=0, channel_multiplier=-1), waits=[t_mk0])

        t_pf = [p.dma("sync", lambda e, d=d: e.dma_start(
            out=p_fm[d][:], in_=pfm_in[d * 128:(d + 1) * 128, :])) for d in range(8)]
        t_qf = [p.dma("sync", lambda e, d=d: e.dma_start(
            out=qm_fm[d][:], in_=qm_in[d * 128:(d + 1) * 128, :])) for d in range(8)]
        t_pt = [p.dma("scalar", lambda e, i=i: e.dma_start(
            out=p_tm[i][:], in_=ptm_in[i * 128:(i + 1) * 128, :])) for i in range(NCH)]
        t_wm = cast_load_weight(p, wm_in, wm_sb)
        t_tf = [p.dma("gpsimd", lambda e, h=h: e.dma_start(
            out=trf_sb[(h % 2) * 64:(h % 2 + 1) * 64,
                       (h // 2) * KA:(h // 2 + 1) * KA],
            in_=trf_in[h * KA:(h + 1) * KA, :])) for h in range(NHEAD_A)]
        t_ms = p.dma("sync", lambda e: e.dma_start(out=mrow[:], in_=msk_in))
        t_mb0 = p.comp("tensor", lambda e: e.matmul(
            ps_S[0][0:128, 0:N_CORES], ones_row[:], mrow[:],
            start=True, stop=True), waits=[t_ms, t_or])
        t_mb = p.comp("vector", lambda e: e.tensor_copy(
            m_bc[:], ps_S[0][0:128, 0:N_CORES]), waits=[t_mb0])

        t_wt = transpose_weight(p, nc, E, wm_sb, wmT, t_wm, ident, t_id, ps_tp)

        t_sb = {}
        for b in range(B):
            prev = None
            for c in range(N_CORES):
                tb = tld[c % 2]
                lw = [t_sb[(b, c - 2)]] if (b, c - 2) in t_sb else []
                if b > 0 and c < 2:
                    lw += [t_sb[(b - 1, N_CORES - 2 + c)]]
                t_tl = p.dma("sync", lambda e, b=b, c=c, tb=tb: e.dma_start(
                    out=tb[:], in_=tpk_in[(c * B + b) * 128:(c * B + b + 1) * 128, :]),
                    waits=lw)
                if c == 0:
                    t_ac = p.comp("vector", lambda e, b=b, c=c, tb=tb: e.tensor_scalar(
                        S32[b][:], tb[:], m_bc[:, c:c + 1], None, ALU.mult),
                        waits=[t_tl, t_mb] + ([prev] if prev else []))
                else:
                    t_ac = p.comp("vector", lambda e, b=b, c=c, tb=tb: e.scalar_tensor_tensor(
                        S32[b][:], tb[:], m_bc[:, c:c + 1], S32[b][:],
                        ALU.mult, ALU.add), waits=[t_tl, prev])
                t_sb[(b, c)] = t_ac
                prev = t_ac
            t_sb[b] = p.comp("vector", lambda e, b=b: e.tensor_copy(
                Sbf[b][:], S32[b][:]), waits=[prev])

        z_toks = []
        last = {}
        for b in range(B):
            s_tok = t_sb[b]
            s32_tok = None
            for ch in range(8):
                col0 = b * SLOC + ch * 128
                for n in range(NHEAD_A):
                    r0 = (n % 2) * 64
                    cb = (n // 2) * KA
                    hp = n // 2
                    pb = (b * 8 + ch) * NHEAD_A + n
                    psS = ps_S[pb % 2]
                    psY = ps_y[pb % 2]
                    psZ = psS
                    w0 = [t_pf[hp], t_qf[hp], t_mk]
                    if ("S", pb % 2) in last:
                        w0.append(last[("S", pb % 2)])
                    tS = p.comp("tensor", lambda e, hp=hp, r0=r0, col0=col0, psS=psS: e.matmul(
                        psS[:], p_fm[hp][r0:r0 + 64, col0:col0 + 128],
                        qm_fm[hp][r0:r0 + 64, col0:col0 + 128],
                        start=True, stop=True, tile_position=(r0, 0)), waits=w0)
                    tSm = p.comp("vector", lambda e, psS=psS: e.tensor_tensor(
                        spm[:], psS[:], maskU[:], ALU.mult),
                        waits=[tS] + ([last[("spm", 0)]] if ("spm", 0) in last else []))
                    last[("S", pb % 2)] = tSm
                    w1 = [tSm, t_pt[b * 8 + ch]]
                    if ("Y", pb % 2) in last:
                        w1.append(last[("Y", pb % 2)])
                    tI = p.comp("tensor", lambda e, b=b, ch=ch, n=n, r0=r0, psY=psY: e.matmul(
                        psY[r0:r0 + 64, :], p_tm[b * 8 + ch][:, n * KA:(n + 1) * KA],
                        spm[:], start=True, stop=False, tile_position=(0, r0)),
                        waits=w1)
                    tJ = p.comp("tensor", lambda e, b=b, hp=hp, r0=r0, cb=cb, col0=col0, psY=psY: e.matmul(
                        psY[r0:r0 + 64, :], Sbf[b][r0:r0 + 64, cb:cb + KA],
                        qm_fm[hp][r0:r0 + 64, col0:col0 + 128],
                        start=False, stop=True, tile_position=(r0, r0)),
                        waits=[s_tok])
                    tN = p.comp("vector", lambda e, r0=r0, psY=psY: e.tensor_copy(
                        nud[r0:r0 + 64, :], psY[r0:r0 + 64, :]),
                        waits=[tJ, tI] + ([last[("nud", r0)]] if ("nud", r0) in last else []))
                    last[("Y", pb % 2)] = tN
                    w2 = [tN, t_tf[n], tSm]
                    tZ = p.comp("tensor", lambda e, n=n, r0=r0, cb=cb, psZ=psZ: e.matmul(
                        psZ[r0:r0 + 64, :], trf_sb[r0:r0 + 64, cb:cb + KA],
                        nud[r0:r0 + 64, :], start=True, stop=True,
                        tile_position=(r0, r0)), waits=w2)
                    tZe = p.comp("vector", lambda e, hp=hp, r0=r0, col0=col0, psZ=psZ: e.tensor_copy(
                        z_fm[hp][r0:r0 + 64, col0:col0 + 128], psZ[r0:r0 + 64, :]),
                        waits=[tZ])
                    last[("nud", r0)] = tZ
                    z_toks.append(tZe)
                    w3 = [t_pt[b * 8 + ch], tJ]
                    if ("O", pb % 2) in last:
                        w3.append(last[("O", pb % 2)])
                    tO = p.comp("tensor", lambda e, b=b, ch=ch, n=n, r0=r0, psS=psS: e.matmul(
                        psS[r0:r0 + 64, 0:KA],
                        p_tm[b * 8 + ch][:, n * KA:(n + 1) * KA],
                        p_tm[b * 8 + ch][:, n * KA:(n + 1) * KA],
                        start=True, stop=True, tile_position=(0, r0)),
                        waits=w3 + [tZe])
                    tSa = p.comp("vector", lambda e, b=b, r0=r0, cb=cb, psS=psS: e.tensor_tensor(
                        S32[b][r0:r0 + 64, cb:cb + KA],
                        S32[b][r0:r0 + 64, cb:cb + KA],
                        psS[r0:r0 + 64, 0:KA], ALU.add),
                        waits=[tO, s32_tok if s32_tok else s_tok])
                    tSb = p.comp("vector", lambda e, b=b, r0=r0, cb=cb: e.tensor_copy(
                        Sbf[b][r0:r0 + 64, cb:cb + KA],
                        S32[b][r0:r0 + 64, cb:cb + KA]), waits=[tSa])
                    last[("O", pb % 2)] = tSa
                    s_tok = tSb
                    s32_tok = tSa

        bank_tok = {}
        t_y = []
        slot_tok = {}
        for i in range(NCH):
            sl = i % 2
            xb = x_sb[sl]
            ob = o_sb[sl]
            guard = [slot_tok[sl]] if sl in slot_tok else []
            t_xr = p.dma("sync", lambda e, i=i, xb=xb: e.dma_start(
                out=xb[:], in_=x_in[i * 128:(i + 1) * 128, :]), waits=guard)
            ev = []
            for h in range(2):
                bk = ps_mm[h]
                mm = None
                for d in range(8):
                    w0 = [t_wt[-1], z_toks[-1]]
                    if d == 0 and id(bk) in bank_tok:
                        w0.append(bank_tok[id(bk)])
                    mm = p.comp("tensor", lambda e, i=i, d=d, h=h, bk=bk: e.matmul(
                        bk[:], z_fm[d][:, i * 128:(i + 1) * 128],
                        wmT[d][:, h * 512:(h + 1) * 512],
                        start=(d == 0), stop=(d == 7)), waits=w0)
                t_e = p.comp("vector", lambda e, h=h, bk=bk, ob=ob, xb=xb: e.tensor_tensor(
                    ob[:, h * 512:(h + 1) * 512], bk[:],
                    xb[:, h * 512:(h + 1) * 512], ALU.add),
                    waits=[mm, t_xr] + guard)
                bank_tok[id(bk)] = t_e
                ev.append(t_e)
            t_st = p.dma("sync", lambda e, i=i, ob=ob: e.dma_start(
                out=y_out[i * 128:(i + 1) * 128, :], in_=ob[:]), waits=ev)
            slot_tok[sl] = t_st
            t_y.append(t_st)
        p.raw("sync", lambda e: e.nop(), waits=p.barrier_tokens())
        p.emit()
    return nc


def run_stage_E(x_shards, w_proj, pre_metric, transforms, w_mixer):
    nce1 = cached_nc("E1", build_launch_E1)
    pm = np.ascontiguousarray(pre_metric.reshape(NHEAD_A * KA, KA))
    res1 = run_bass_kernel_spmd(
        nce1, [dict(x=x_shards[c], wp=w_proj, pm=pm) for c in range(N_CORES)],
        core_ids=list(range(N_CORES)))
    o1 = res1.results
    tpk = np.concatenate([o1[c]["tc"] for c in range(N_CORES)], axis=0)
    nce2 = cached_nc("E2", build_launch_E2)
    trf = np.ascontiguousarray(transforms.reshape(NHEAD_A * KA, KA))
    in2 = []
    for c in range(N_CORES):
        mask = np.zeros((1, N_CORES), np.float32)
        mask[0, :c] = 1.0
        in2.append(dict(x=x_shards[c], pfm=o1[c]["pfm"], ptm=o1[c]["ptm"],
                        qmf=o1[c]["qmf"], tpk=tpk, mcmask=mask, trf=trf,
                        wm=w_mixer))
    res2 = run_bass_kernel_spmd(nce2, in2, core_ids=list(range(N_CORES)))
    return [res2.results[c]["y"] for c in range(N_CORES)]


DST = 64
HDIM = 32
NH_B = 64
DIN = 2048
KPROJ = 4432
A_FLOOR = 1e-4
MAGIC = 12582912.0
TWO_PI = float(2 * np.pi)
HALF_PI = float(np.pi / 2)


def build_launch_G1a():
    nc = new_nc()
    x_in = nc.dram_tensor("x", [TLOC, D], F32, kind="ExternalInput").ap()
    win_in = nc.dram_tensor("win", [KPROJ, D], F32, kind="ExternalInput").ap()
    dtb_in = nc.dram_tensor("dtb", [1, NH_B], F32, kind="ExternalInput").ap()
    bb_in = nc.dram_tensor("bb", [NH_B, DST], F32, kind="ExternalInput").ap()
    cb_in = nc.dram_tensor("cb", [NH_B, DST], F32, kind="ExternalInput").ap()
    bnw_in = nc.dram_tensor("bnw", [1, DST], F32, kind="ExternalInput").ap()
    cnw_in = nc.dram_tensor("cnw", [1, DST], F32, kind="ExternalInput").ap()

    z_out = nc.dram_tensor("zo", [DIN, TLOC], BF16, kind="ExternalOutput").ap()
    xfm_out = nc.dram_tensor("xfo", [DIN, TLOC], BF16, kind="ExternalOutput").ap()
    xtm_out = nc.dram_tensor("xto", [TLOC, DIN], BF16, kind="ExternalOutput").ap()
    sml_out = nc.dram_tensor("smo", [6 * 64, TLOC], F32, kind="ExternalOutput").ap()

    stage_w = nc.dram_tensor("stgw", [KPROJ, D], BF16, kind="Internal").ap()

    KO = {"z": 0, "x": 2048, "B": 4096, "C": 4160, "dt": 4224, "A": 4288,
          "tr": 4352, "ang": 4416}

    with ExitStack() as ctx, Prog(nc) as p:
        E = ctx.enter_context
        x_sb = [sb(E, nc, f"x{i}", [128, D], F32) for i in range(2)]
        xh_tm = [sb(E, nc, f"xh{i}", [128, D], BF16) for i in range(2)]
        xh_fm = [sb(E, nc, f"xf{d}", [128, TLOC], BF16) for d in range(8)]
        stat = sb(E, nc, "stat", [128, 8 * NCH], F32)
        sq_scr = [sb(E, nc, f"sqs{i}", [128, D], F32) for i in range(2)]
        cb_sb = [sb(E, nc, f"cbb{i}", [128, D], BF16) for i in range(2)]
        winT = [sb(E, nc, f"wtt{d}", [128, KPROJ], BF16) for d in range(8)]
        obuf = [sb(E, nc, f"ob{i}", [128, TLOC], BF16) for i in range(2)]
        smls = sb(E, nc, "smls", [64, TLOC], F32)
        ident, t_id = build_identity(p, nc, E)
        cpool = ConstPool(p, nc, E)
        t_eps = cpool.reg(EPS)

        ps_tp = [ps(E, nc, f"pst{i}", [128, 128], BF16) for i in range(2)]
        nc._lnfm_ps = ps_tp
        ps_mm = [ps(E, nc, f"psm{i}", [128, 512], F32) for i in range(4)]

        bank_tok = {}
        t_cst = []
        for r in range(0, KPROJ, 128):
            r1 = min(r + 128, KPROJ)
            bufi = len(t_cst) % 2
            lw = [t_cst[-2][1]] if len(t_cst) >= 2 else []
            t_l = p.dma("gpsimd", lambda e, r=r, r1=r1, bufi=bufi: e.dma_start(
                out=cb_sb[bufi][0:r1 - r, :], in_=win_in[r:r1, :]), waits=lw)
            t_s = p.dma("scalar", lambda e, r=r, r1=r1, bufi=bufi: e.dma_start(
                out=stage_w[r:r1, :], in_=cb_sb[bufi][0:r1 - r, :]), waits=[t_l])
            t_cst.append((t_l, t_s))
        t_wt = []
        for d in range(8):
            for kb in range(0, KPROJ, 512):
                k1 = min(kb + 512, KPROJ)
                t_wt.append(p.dma("sync", lambda e, d=d, kb=kb, k1=k1: e.dma_start(
                    out=winT[d][:, kb:k1],
                    in_=stage_w[kb:k1, d * 128:(d + 1) * 128], transpose=True),
                    waits=[t_cst[-1][1]]))
        t_wL = t_wt[-1]

        t_ld, t_xh, t_fm_last = emit_ln_and_fm(
            p, nc, E, x_in, x_sb, xh_tm, xh_fm, stat, sq_scr, ident, t_id, t_eps)


        def wproj_rows(col0, nrows, evac, extra_w=()):
            toks = []
            for tt in range(4):
                bk = ps_mm[tt % 4]
                mm = None
                for d in range(8):
                    w0 = [t_wL, t_fm_last] + list(extra_w)
                    if d == 0 and id(bk) in bank_tok:
                        w0.append(bank_tok[id(bk)])
                    mm = p.comp("tensor", lambda e, col0=col0, nrows=nrows, tt=tt, d=d, bk=bk: e.matmul(
                        bk[0:nrows, :], winT[d][:, col0:col0 + nrows],
                        xh_fm[d][:, tt * 512:(tt + 1) * 512],
                        start=(d == 0), stop=(d == 7)), waits=w0)
                t_e = evac(tt, bk, mm)
                bank_tok[id(bk)] = t_e
                toks.append(t_e)
            return toks

        t_zx = []
        for part, out_d in (("z", z_out), ("x", xfm_out)):
            for kc in range(16):
                ob = obuf[kc % 2]
                evs = []

                def ev(tt, bk, mm, ob=ob):
                    w = [mm] + ([t_zx[-2]] if len(t_zx) >= 2 else [])
                    t_e = p.comp("vector", lambda e, tt=tt, bk=bk, ob=ob: e.tensor_copy(
                        ob[:, tt * 512:(tt + 1) * 512], bk[0:128, :]), waits=w)
                    evs.append(t_e)
                    return t_e
                wproj_rows(KO[part] + kc * 128, 128, ev)
                t_zx.append(p.dma("scalar", lambda e, kc=kc, ob=ob, out_d=out_d: e.dma_start(
                    out=out_d[kc * 128:(kc + 1) * 128, :], in_=ob[:]), waits=evs))

        srow = {"B": 0, "C": 64, "dt": 128, "A": 192, "tr": 256, "ang": 320}
        t_sml = []
        for key, nr in (("B", 64), ("C", 64), ("dt", 64), ("A", 64),
                        ("tr", 64), ("ang", 16)):
            evs = []

            def ev(tt, bk, mm, nr=nr):
                w = [mm] + ([t_sml[-1]] if t_sml else [])
                t_e = p.comp("vector", lambda e, tt=tt, bk=bk, nr=nr: e.tensor_copy(
                    smls[0:nr, tt * 512:(tt + 1) * 512], bk[0:nr, :]), waits=w)
                evs.append(t_e)
                return t_e
            wproj_rows(KO[key], nr, ev)
            t_sml.append(p.dma("sync", lambda e, key=key, nr=nr: e.dma_start(
                out=sml_out[srow[key]:srow[key] + nr, :], in_=smls[0:nr, :]),
                waits=evs))
        xtm_sb = sb(E, nc, "xtmsb", [128, DIN], BF16)
        t_prev_x = None
        for i in range(NCH):
            evs = []
            for tt in range(4):
                bk = ps_mm[tt % 4]
                mm = None
                for d in range(8):
                    w0 = [t_wL, t_fm_last]
                    if d == 0:
                        w0.append(bank_tok.get(id(bk)))
                    mm = p.comp("tensor", lambda e, i=i, tt=tt, d=d, bk=bk: e.matmul(
                        bk[:], xh_fm[d][:, i * 128:(i + 1) * 128],
                        winT[d][:, KO["x"] + tt * 512:KO["x"] + (tt + 1) * 512],
                        start=(d == 0), stop=(d == 7)),
                        waits=[w for w in w0 if w])
                t_e = p.comp("vector", lambda e, tt=tt, bk=bk: e.tensor_copy(
                    xtm_sb[:, tt * 512:(tt + 1) * 512], bk[:]),
                    waits=[mm] + ([t_prev_x] if t_prev_x and tt == 0 else []))
                bank_tok[id(bk)] = t_e
                evs.append(t_e)
            t_prev_x = p.dma("scalar", lambda e, i=i: e.dma_start(
                out=xtm_out[i * 128:(i + 1) * 128, :], in_=xtm_sb[:]), waits=evs)
        p.raw("sync", lambda e: e.nop(), waits=p.barrier_tokens())
        p.emit()
    return nc


def build_launch_G1b():
    nc = new_nc()
    sml_in = nc.dram_tensor("smi", [6 * 64, TLOC], F32, kind="ExternalInput").ap()
    xtm_in = nc.dram_tensor("xti", [TLOC, DIN], BF16, kind="ExternalInput").ap()
    dtb_in = nc.dram_tensor("dtb", [1, NH_B], F32, kind="ExternalInput").ap()
    bb_in = nc.dram_tensor("bb", [NH_B, DST], F32, kind="ExternalInput").ap()
    cb_in = nc.dram_tensor("cb", [NH_B, DST], F32, kind="ExternalInput").ap()
    bnw_in = nc.dram_tensor("bnw", [1, DST], F32, kind="ExternalInput").ap()
    cnw_in = nc.dram_tensor("cnw", [1, DST], F32, kind="ExternalInput").ap()

    la_out = nc.dram_tensor("lao", [NH_B, TLOC], F32, kind="ExternalOutput").ap()
    gdt_out = nc.dram_tensor("gdo", [NH_B, TLOC], F32, kind="ExternalOutput").ap()
    bmt_out = nc.dram_tensor("bmo", [TLOC, NH_B * DST], BF16, kind="ExternalOutput").ap()
    cmt_out = nc.dram_tensor("cmo", [TLOC, NH_B * DST], BF16, kind="ExternalOutput").ap()
    ac_out = nc.dram_tensor("aco", [B, NH_B], F32, kind="ExternalOutput").ap()
    bc_out = nc.dram_tensor("bco", [B * DST, NH_B * HDIM], F32, kind="ExternalOutput").ap()

    bias_bounce = nc.dram_tensor("bbon", [4, NH_B * DST], F32, kind="Internal").ap()

    with ExitStack() as ctx, Prog(nc) as p:
        E = ctx.enter_context
        Braw = sb(E, nc, "braw", [64, TLOC], F32)
        Craw = sb(E, nc, "craw", [64, TLOC], F32)
        ddt = sb(E, nc, "ddt", [64, TLOC], F32)
        ddA = sb(E, nc, "dda", [64, TLOC], F32)
        trp = sb(E, nc, "trp", [64, TLOC], F32)
        angs = sb(E, nc, "angs", [16, TLOC], F32)
        ew1 = sb(E, nc, "ew1", [64, TLOC], F32)
        ew2 = sb(E, nc, "ew2", [64, TLOC], F32)
        dt_fm = sb(E, nc, "dtfm", [64, TLOC], F32)
        la_fm = sb(E, nc, "lafm", [64, TLOC], F32)
        gdt_fm = sb(E, nc, "gdfm", [64, TLOC], F32)
        cos_fm = sb(E, nc, "cosf", [16, TLOC], F32)
        sin_fm = sb(E, nc, "sinf", [16, TLOC], F32)
        rmsB = sb(E, nc, "rmsb", [64, TLOC], BF16)
        rmsC = sb(E, nc, "rmsc", [64, TLOC], BF16)
        nrm_row = sb(E, nc, "nrmr", [1, TLOC], F32)
        bbs = sb(E, nc, "bbs", [64, DST], F32)
        jbs = sb(E, nc, "jbs", [64, DST], F32)
        dtb_c = sb(E, nc, "dtbc", [64, 1], F32)
        nw_c = sb(E, nc, "nwc", [64, 2], F32)
        ones_row = sb(E, nc, "onesr", [1, 128], F32)
        ones_colf = sb(E, nc, "onescf", [128, 1], F32)
        biasB_bc = sb(E, nc, "bbcB", [128, NH_B * DST], BF16)
        jbB_bc = sb(E, nc, "jbcB", [128, NH_B * DST], BF16)
        biasC_bc = sb(E, nc, "bbcC", [128, NH_B * DST], BF16)
        jbC_bc = sb(E, nc, "jbcC", [128, NH_B * DST], BF16)
        bmt_one = sb(E, nc, "bmtb", [128, NH_B * DST], BF16)
        cmt_one = sb(E, nc, "cmtb", [128, NH_B * DST], BF16)
        bmt_buf = [bmt_one, bmt_one]
        cmt_buf = [cmt_one, cmt_one]
        la_tm = sb(E, nc, "latm", [128, 64 * 8], F32)
        gdt_tm = [sb(E, nc, f"gdtt{i}", [128, 64], F32) for i in range(2)]
        wt_bf = [sb(E, nc, f"wtb{i}", [128, 64], BF16) for i in range(2)]
        rmB_tm = [sb(E, nc, f"rbt{i}", [128, 64], BF16) for i in range(2)]
        rmC_tm = [sb(E, nc, f"rct{i}", [128, 64], BF16) for i in range(2)]
        jrB_tm = [sb(E, nc, f"jbt{i}", [128, 64], BF16) for i in range(2)]
        jrC_tm = [sb(E, nc, f"jct{i}", [128, 64], BF16) for i in range(2)]
        cs_tm = [sb(E, nc, f"cst{i}", [128, 16], F32) for i in range(2)]
        sn_tm = [sb(E, nc, f"snt{i}", [128, 16], F32) for i in range(2)]
        ctil = [sb(E, nc, f"ctl{i}", [128, 64], BF16) for i in range(2)]
        stil = [sb(E, nc, f"stl{i}", [128, 64], BF16) for i in range(2)]
        xtm_buf = [sb(E, nc, f"xtb{i}", [128, DIN], BF16) for i in range(2)]
        xw_one = sb(E, nc, "xwb", [128, DIN], BF16)
        xw_buf = [xw_one, xw_one]
        scr_tm = sb(E, nc, "scrtm", [128, 64], F32)
        carry = sb(E, nc, "carry", [1, NH_B], F32)
        csum_r = sb(E, nc, "csumr", [1, NH_B], F32)
        ltot_bc = sb(E, nc, "ltotb", [128, 64], F32)
        spack = [sb(E, nc, f"spk{b}", [64, NH_B * HDIM], F32) for b in range(B)]
        acr = sb(E, nc, "acr", [1, NH_B], F32)
        trilI = sb(E, nc, "trilI", [128, 128], F32)
        identf = sb(E, nc, "identf", [128, 128], F32)
        ident, t_id = build_identity(p, nc, E)
        cpool = ConstPool(p, nc, E)
        t_eps = cpool.reg(EPS)
        t_one = cpool.reg(1.0)

        ps_tp = [ps(E, nc, f"pst{i}", [128, 128], BF16) for i in range(2)]
        ps_mm = [ps(E, nc, f"psm{i}", [128, 512], F32) for i in range(4)]
        ps_st = ps(E, nc, "psst", [64, 64], F32)
        ps_nr = ps(E, nc, "psnr", [1, 512], F32)

        bank_tok = {}
        t_or = p.comp("gpsimd", lambda e: e.memset(ones_row[:], 1.0))
        t_ocf = p.comp("gpsimd", lambda e: e.memset(ones_colf[:], 1.0))
        t_ti0 = p.comp("gpsimd", lambda e: e.memset(trilI[:], 1.0))
        t_ti = p.comp("gpsimd", lambda e: e.affine_select(
            trilI[:], trilI[:], pattern=[[1, 128]], compare_op=ALU.is_ge,
            fill=0.0, base=0, channel_multiplier=-1), waits=[t_ti0])
        t_if0 = p.comp("gpsimd", lambda e: e.memset(identf[:], 1.0))
        t_if = p.comp("gpsimd", lambda e: e.affine_select(
            identf[:], identf[:], pattern=[[1, 128]], compare_op=ALU.is_equal,
            fill=0.0, base=0, channel_multiplier=-1), waits=[t_if0])
        t_cz = p.comp("gpsimd", lambda e: e.memset(carry[:], 0.0))

        t_projS = None
        for key, dstt, r0, nr in (("B", Braw, 0, 64), ("C", Craw, 64, 64),
                                  ("dt", ddt, 128, 64), ("A", ddA, 192, 64),
                                  ("tr", trp, 256, 64), ("ang", angs, 320, 16)):
            t_projS = p.dma("sync", lambda e, dstt=dstt, r0=r0, nr=nr: e.dma_start(
                out=dstt[0:nr, :], in_=sml_in[r0:r0 + nr, :]))
        t_dtb = p.dma("sync", lambda e: e.dma_start(
            out=dtb_c[:], in_=dtb_in.rearrange("a b -> b a"),
            allow_slow_non_contiguous=True))
        t_nw = p.dma("sync", lambda e: e.dma_start(
            out=nw_c[:, 0:1], in_=bnw_in.rearrange("a b -> b a"),
            allow_slow_non_contiguous=True))
        t_nw2 = p.dma("sync", lambda e: e.dma_start(
            out=nw_c[:, 1:2], in_=cnw_in.rearrange("a b -> b a"),
            allow_slow_non_contiguous=True))

        def mk_bias_tables(src_in, row0, dstB, dstJ, prev=None):
            t_b = p.dma("sync", lambda e: e.dma_start(out=bbs[:], in_=src_in))
            tj1 = p.comp("vector", lambda e: e.tensor_scalar(
                jbs[:, 0:16], bbs[:, 16:32], -1.0, None, ALU.mult), waits=[t_b])
            tj2 = p.comp("vector", lambda e: e.tensor_copy(
                jbs[:, 16:32], bbs[:, 0:16]), waits=[tj1])
            tj3 = p.comp("vector", lambda e: e.memset(jbs[:, 32:64], 0.0),
                         waits=[tj2])
            tw1 = p.dma("scalar", lambda e: e.dma_start(
                out=bias_bounce[row0:row0 + 1, :].rearrange(
                    "o (h n) -> (o h) n", h=NH_B),
                in_=bbs[:]), waits=[t_b])
            tw2 = p.dma("scalar", lambda e: e.dma_start(
                out=bias_bounce[row0 + 1:row0 + 2, :].rearrange(
                    "o (h n) -> (o h) n", h=NH_B),
                in_=jbs[:]), waits=[tj3])
            toks = []
            for cc in range(8):
                bk = ps_mm[cc % 4]
                tr_ = p.dma("sync", lambda e, cc=cc, row0=row0: e.dma_start(
                    out=nrm_row[0:1, 0:512],
                    in_=bias_bounce[row0:row0 + 1, cc * 512:(cc + 1) * 512]),
                    waits=[tw1] + ([toks[-1]] if toks else ([prev] if prev else [])))
                tm_ = p.comp("tensor", lambda e, bk=bk: e.matmul(
                    bk[:], ones_row[:], nrm_row[0:1, 0:512],
                    start=True, stop=True), waits=[tr_, t_or])
                te_ = p.comp("vector", lambda e, cc=cc, bk=bk, dstB=dstB: e.tensor_copy(
                    dstB[:, cc * 512:(cc + 1) * 512], bk[:]), waits=[tm_])
                toks.append(te_)
            for cc in range(8):
                bk = ps_mm[cc % 4]
                tr_ = p.dma("sync", lambda e, cc=cc, row0=row0: e.dma_start(
                    out=nrm_row[0:1, 0:512],
                    in_=bias_bounce[row0 + 1:row0 + 2, cc * 512:(cc + 1) * 512]),
                    waits=[tw2, toks[-1]])
                tm_ = p.comp("tensor", lambda e, bk=bk: e.matmul(
                    bk[:], ones_row[:], nrm_row[0:1, 0:512],
                    start=True, stop=True), waits=[tr_])
                te_ = p.comp("vector", lambda e, cc=cc, bk=bk, dstJ=dstJ: e.tensor_copy(
                    dstJ[:, cc * 512:(cc + 1) * 512], bk[:]), waits=[tm_])
                toks.append(te_)
            return toks[-1]

        t_btabB = mk_bias_tables(bb_in, 0, biasB_bc, jbB_bc)
        t_btabC = mk_bias_tables(cb_in, 2, biasC_bc, jbC_bc, prev=t_btabB)

        def softplus64(src, bias_col, dst, waits):
            if bias_col is not None:
                tu = p.comp("vector", lambda e: e.tensor_scalar(
                    ew1[:], src, bias_col, None, ALU.add), waits=waits)
            else:
                tu = p.comp("vector", lambda e: e.tensor_copy(ew1[:], src),
                            waits=waits)
            ta_ = p.comp("scalar", lambda e: e.activation(
                ew2[:], ew1[:], AF.Abs), waits=[tu])
            tb_ = p.comp("scalar", lambda e: e.activation(
                ew2[:], ew2[:], AF.Exp, bias=0.0, scale=-1.0), waits=[ta_])
            tc_ = p.comp("scalar", lambda e: e.activation(
                ew2[:], ew2[:], AF.Ln, bias=1.0, scale=1.0),
                waits=[tb_, t_one])
            td_ = p.comp("vector", lambda e: e.tensor_scalar(
                ew1[:], ew1[:], 0.0, None, ALU.max), waits=[tc_])
            return p.comp("vector", lambda e, dst=dst: e.tensor_tensor(
                dst, ew1[:], ew2[:], ALU.add), waits=[td_])

        t_dt = softplus64(ddt[:], dtb_c[:, 0:1], dt_fm[:], [t_projS, t_dtb])
        t_sp2 = softplus64(ddA[:], None, ew1[:] if False else la_fm[:], [t_dt])
        t_m = p.comp("vector", lambda e: e.tensor_scalar(
            la_fm[:], la_fm[:], A_FLOOR, None, ALU.max), waits=[t_sp2])
        t_ad = p.comp("vector", lambda e: e.tensor_tensor(
            la_fm[:], la_fm[:], dt_fm[:], ALU.mult), waits=[t_m])
        t_ea = p.comp("scalar", lambda e: e.activation(
            la_fm[:], la_fm[:], AF.Exp, bias=0.0, scale=-1.0), waits=[t_ad])
        t_tr = p.comp("scalar", lambda e: e.activation(
            trp[:], trp[:], AF.Sigmoid), waits=[t_projS])
        t_g = p.comp("vector", lambda e: e.tensor_scalar(
            ew1[:], trp[:], -0.5, 1.0, ALU.mult, ALU.add), waits=[t_tr])
        t_a = p.comp("vector", lambda e: e.tensor_tensor(
            la_fm[:], la_fm[:], ew1[:], ALU.mult), waits=[t_g, t_ea])
        t_a2 = p.comp("vector", lambda e: e.scalar_tensor_tensor(
            la_fm[:], trp[:], 0.5, la_fm[:], ALU.mult, ALU.add), waits=[t_a])
        t_la = p.comp("scalar", lambda e: e.activation(
            la_fm[:], la_fm[:], AF.Ln), waits=[t_a2])
        t_gdt = p.comp("vector", lambda e: e.tensor_tensor(
            gdt_fm[:], ew1[:], dt_fm[:], ALU.mult), waits=[t_la])
        p.dma("sync", lambda e: e.dma_start(out=la_out[0:64, :], in_=la_fm[:]),
              waits=[t_la])
        p.dma("sync", lambda e: e.dma_start(out=gdt_out[0:64, :], in_=gdt_fm[:]),
              waits=[t_gdt])

        t_w1 = p.comp("vector", lambda e: e.tensor_scalar(
            cos_fm[:], angs[:], 1.0 / TWO_PI, MAGIC, ALU.mult, ALU.add),
            waits=[t_projS])
        t_w2 = p.comp("vector", lambda e: e.tensor_scalar(
            cos_fm[:], cos_fm[:], MAGIC, -TWO_PI, ALU.subtract, ALU.mult),
            waits=[t_w1])
        t_w3 = p.comp("vector", lambda e: e.scalar_tensor_tensor(
            sin_fm[:], angs[:], 1.0, cos_fm[:], ALU.mult, ALU.add),
            waits=[t_w2])
        t_sn = p.comp("scalar", lambda e: e.activation(
            sin_fm[:], sin_fm[:], AF.Sin), waits=[t_w3])
        t_c1 = p.comp("vector", lambda e: e.tensor_scalar(
            ew1[0:16, :], angs[:], HALF_PI, None, ALU.add), waits=[t_sn])
        t_c2 = p.comp("vector", lambda e: e.tensor_scalar(
            cos_fm[:], ew1[0:16, :], 1.0 / TWO_PI, MAGIC, ALU.mult, ALU.add),
            waits=[t_c1])
        t_c3 = p.comp("vector", lambda e: e.tensor_scalar(
            cos_fm[:], cos_fm[:], MAGIC, -TWO_PI, ALU.subtract, ALU.mult),
            waits=[t_c2])
        t_c4 = p.comp("vector", lambda e: e.scalar_tensor_tensor(
            cos_fm[:], ew1[0:16, :], 1.0, cos_fm[:], ALU.mult, ALU.add),
            waits=[t_c3])
        t_cs = p.comp("scalar", lambda e: e.activation(
            cos_fm[:], cos_fm[:], AF.Sin), waits=[t_c4])

        def rms_side(src, nw_col, dst, waits):
            tq = p.comp("scalar", lambda e, src=src: e.activation(
                ew1[:], src, AF.Square), waits=waits)
            tms = []
            for tt in range(4):
                w_ = [tq, t_ocf] + ([tms[-1]] if tms else [])
                mm_ = p.comp("tensor", lambda e, tt=tt: e.matmul(
                    ps_nr[0:1, 0:512], ones_colf[0:64, :],
                    ew1[:, tt * 512:(tt + 1) * 512],
                    start=True, stop=True), waits=w_)
                tms.append(p.comp("vector", lambda e, tt=tt: e.tensor_copy(
                    nrm_row[0:1, tt * 512:(tt + 1) * 512], ps_nr[0:1, 0:512]),
                    waits=[mm_]))
            t1 = p.comp("vector", lambda e: e.tensor_scalar(
                nrm_row[0:1, :], nrm_row[0:1, :], 1.0 / DST, EPS,
                ALU.mult, ALU.add), waits=[tms[-1], t_eps])
            t2 = p.comp("scalar", lambda e: e.activation(
                nrm_row[0:1, :], nrm_row[0:1, :], AF.Sqrt), waits=[t1])
            t3 = p.comp("vector", lambda e: e.reciprocal(
                nrm_row[0:1, :], nrm_row[0:1, :]), waits=[t2])
            outs = []
            for tt in range(4):
                bk = ps_mm[tt % 4]
                w0 = [t3]
                if id(bk) in bank_tok:
                    w0.append(bank_tok[id(bk)])
                t4 = p.comp("tensor", lambda e, tt=tt, bk=bk: e.matmul(
                    bk[0:64, :], ones_row[0:1, 0:64],
                    nrm_row[0:1, tt * 512:(tt + 1) * 512],
                    start=True, stop=True), waits=w0)
                t5 = p.comp("vector", lambda e, tt=tt, bk=bk, src=src, nw_col=nw_col, dst=dst: e.scalar_tensor_tensor(
                    dst[:, tt * 512:(tt + 1) * 512],
                    src[:, tt * 512:(tt + 1) * 512] if True else None,
                    nw_col, bk[0:64, :], ALU.mult, ALU.mult), waits=[t4])
                bank_tok[id(bk)] = t5
                outs.append(t5)
            return outs[-1]

        t_rB = rms_side(Braw[:], nw_c[:, 0:1], rmsB, [t_projS, t_nw, t_btabC])
        t_rC = rms_side(Craw[:], nw_c[:, 1:2], rmsC, [t_rB, t_nw2])

        prev_bst = None
        prev_cst = None
        t_cum = {}
        t_carry = t_cz
        ac_toks = []
        prev_batch = []
        for b in range(B):
            t_cz2 = p.comp("gpsimd", lambda e: e.memset(carry[:], 0.0),
                           waits=([t_carry] + prev_batch) if b else [t_cz])
            t_carry = t_cz2
            for ch in range(8):
                cidx = b * 8 + ch
                col0 = b * SLOC + ch * 128
                pt_f = ps_mm[cidx % 2]
                w0 = [t_la, t_if, bank_tok.get(id(pt_f))]
                t_t1 = p.comp("tensor", lambda e, col0=col0, pt_f=pt_f: e.transpose(
                    pt_f[0:128, 0:64].bitcast(F32) if False else pt_f[0:128, 0:64],
                    la_fm[:, col0:col0 + 128], identf[0:64, 0:64]),
                    waits=[w for w in w0 if w])
                t_t2 = p.comp("vector", lambda e, pt_f=pt_f: e.tensor_copy(
                    scr_tm[:], pt_f[0:128, 0:64]), waits=[t_t1])
                bank_tok[id(pt_f)] = t_t2
                bk2 = ps_mm[2 + cidx % 2]
                w1 = [t_t2, t_ti, bank_tok.get(id(bk2))]
                t_t3 = p.comp("tensor", lambda e, bk2=bk2: e.matmul(
                    bk2[0:128, 0:64], trilI[:], scr_tm[:],
                    start=True, stop=True), waits=[w for w in w1 if w])
                bkc = ps_mm[cidx % 2]
                t_t4 = p.comp("tensor", lambda e, bkc=bkc: e.matmul(
                    bkc[0:128, 0:64], ones_row[:], carry[0:1, :],
                    start=True, stop=True), waits=[t_carry, t_t2])
                t_t5 = p.comp("vector", lambda e, bkc=bkc: e.tensor_copy(
                    scr_tm[:], bkc[0:128, 0:64]), waits=[t_t4])
                bank_tok[id(bkc)] = t_t5
                t_t6 = p.comp("vector", lambda e, ch=ch, bk2=bk2: e.tensor_tensor(
                    la_tm[:, ch * 64:(ch + 1) * 64],
                    bk2[0:128, 0:64], scr_tm[:], ALU.add), waits=[t_t3, t_t5])
                bank_tok[id(bk2)] = t_t6
                t_cum[(b, ch)] = t_t6
                bk3 = ps_mm[2 + (cidx + 1) % 2]
                w2 = [t_t2, bank_tok.get(id(bk3))]
                t_l1 = p.comp("scalar", lambda e, col0=col0: e.activation(
                    ew2[:, 0:128], la_fm[:, col0:col0 + 128], AF.Copy,
                    bias=0.0, scale=1.0, accum_out=ew2[:, 128:129]),
                    waits=[t_la, t_t6])
                bkt = ps_tp[cidx % 2]
                t_l2 = p.comp("tensor", lambda e, bkt=bkt: e.transpose(
                    bkt[0:1, 0:64].bitcast(BF16) if False else bkt[0:1, 0:64],
                    ew2[:, 128:129].bitcast(BF16) if False else None,
                    ident[0:64, 0:64]), waits=[t_l1]) if False else None
                bk4 = ps_mm[2 + cidx % 2]
                t_l3 = p.comp("tensor", lambda e, bk4=bk4: e.transpose(
                    bk4[0:1, 0:64], ew2[:, 128:129], identf[0:64, 0:64]),
                    waits=[t_l1, t_t6])
                t_l4 = p.comp("vector", lambda e, bk4=bk4: e.tensor_copy(
                    csum_r[0:1, :], bk4[0:1, 0:64]), waits=[t_l3])
                bank_tok[id(bk4)] = t_l4
                t_carry = p.comp("vector", lambda e: e.tensor_tensor(
                    carry[0:1, :], carry[0:1, :], csum_r[0:1, :], ALU.add),
                    waits=[t_l4])
            t_ac = p.comp("scalar", lambda e: e.activation(
                acr[0:1, :], carry[0:1, :], AF.Exp),
                waits=[t_carry] + (ac_toks[-1:] if ac_toks else []))
            ac_toks.append(p.dma("sync", lambda e, b=b: e.dma_start(
                out=ac_out[b:b + 1, :], in_=acr[0:1, :]), waits=[t_ac]))
            bk5 = ps_mm[0]
            t_lb = p.comp("tensor", lambda e, bk5=bk5: e.matmul(
                bk5[0:128, 0:64], ones_row[:], carry[0:1, :],
                start=True, stop=True), waits=[t_carry, bank_tok.get(id(bk5), t_carry)])
            t_lb2 = p.comp("vector", lambda e, b=b: e.tensor_copy(
                ltot_bc[:], bk5[0:128, 0:64]), waits=[t_lb])
            bank_tok[id(bk5)] = t_lb2

            sp_tok = None
            for ch in range(8):
                cidx = b * 8 + ch
                col0 = b * SLOC + ch * 128
                sl = ch % 2
                xtb = xtm_buf[sl]
                xev = [p.dma("sync", lambda e, cidx=cidx, xtb=xtb: e.dma_start(
                    out=xtb[:], in_=xtm_in[cidx * 128:(cidx + 1) * 128, :]),
                    waits=[sp_tok] if sp_tok else [])]
                trs = {}
                for nm, srcfm, dst in (("rB", rmsB, rmB_tm[sl]), ("rC", rmsC, rmC_tm[sl])):
                    bkt = ps_tp[cidx % 2]
                    w0 = [t_rC, t_id, bank_tok.get(id(bkt))]
                    t_q1 = p.comp("tensor", lambda e, srcfm=srcfm, col0=col0, bkt=bkt: e.transpose(
                        bkt[0:128, 0:64], srcfm[:, col0:col0 + 128],
                        ident[0:64, 0:64]), waits=[w for w in w0 if w])
                    t_q2 = p.comp("vector", lambda e, bkt=bkt, dst=dst: e.tensor_copy(
                        dst[:], bkt[0:128, 0:64]), waits=[t_q1])
                    bank_tok[id(bkt)] = t_q2
                    trs[nm] = t_q2
                bkf = ps_mm[1]
                w0 = [t_gdt, t_if, bank_tok.get(id(bkf))]
                t_q3 = p.comp("tensor", lambda e, col0=col0, bkf=bkf: e.transpose(
                    bkf[0:128, 0:64], gdt_fm[:, col0:col0 + 128],
                    identf[0:64, 0:64]), waits=[w for w in w0 if w])
                t_q4 = p.comp("vector", lambda e, sl=sl, bkf=bkf: e.tensor_copy(
                    gdt_tm[sl][:], bkf[0:128, 0:64]), waits=[t_q3])
                bank_tok[id(bkf)] = t_q4
                w0 = [t_cs, t_sn, bank_tok.get(id(bkf))]
                t_q5 = p.comp("tensor", lambda e, col0=col0, bkf=bkf: e.transpose(
                    bkf[0:128, 0:16], cos_fm[:, col0:col0 + 128],
                    identf[0:16, 0:16]), waits=[w for w in w0 if w])
                t_q6 = p.comp("vector", lambda e, sl=sl, bkf=bkf: e.tensor_copy(
                    cs_tm[sl][:], bkf[0:128, 0:16]), waits=[t_q5])
                t_q7 = p.comp("tensor", lambda e, col0=col0, bkf=bkf: e.transpose(
                    bkf[0:128, 16:32], sin_fm[:, col0:col0 + 128],
                    identf[0:16, 0:16]), waits=[t_q6])
                t_q8 = p.comp("vector", lambda e, sl=sl, bkf=bkf: e.tensor_copy(
                    sn_tm[sl][:], bkf[0:128, 16:32]), waits=[t_q7])
                bank_tok[id(bkf)] = t_q8
                tc1 = p.comp("vector", lambda e, sl=sl: e.tensor_copy(
                    ctil[sl][:, 0:16], cs_tm[sl][:]), waits=[t_q6])
                tc2 = p.comp("vector", lambda e, sl=sl: e.tensor_copy(
                    ctil[sl][:, 16:32], cs_tm[sl][:]), waits=[tc1])
                tc3 = p.comp("vector", lambda e, sl=sl: e.memset(
                    ctil[sl][:, 32:64], 1.0), waits=[tc2])
                ts1 = p.comp("vector", lambda e, sl=sl: e.tensor_copy(
                    stil[sl][:, 0:16], sn_tm[sl][:]), waits=[t_q8])
                ts2 = p.comp("vector", lambda e, sl=sl: e.tensor_copy(
                    stil[sl][:, 16:32], sn_tm[sl][:]), waits=[ts1])
                ts3 = p.comp("vector", lambda e, sl=sl: e.memset(
                    stil[sl][:, 32:64], 0.0), waits=[ts2])
                jr = {}
                for nm, src_tm, dstj in (("B", rmB_tm[sl], jrB_tm[sl]),
                                         ("C", rmC_tm[sl], jrC_tm[sl])):
                    j1 = p.comp("vector", lambda e, src_tm=src_tm, dstj=dstj: e.tensor_scalar(
                        dstj[:, 0:16], src_tm[:, 16:32], -1.0, None, ALU.mult),
                        waits=[trs["r" + nm]])
                    j2 = p.comp("vector", lambda e, src_tm=src_tm, dstj=dstj: e.tensor_copy(
                        dstj[:, 16:32], src_tm[:, 0:16]), waits=[j1])
                    j3 = p.comp("vector", lambda e, dstj=dstj: e.memset(
                        dstj[:, 32:64], 0.0), waits=[j2])
                    jr[nm] = j3
                def build_m(rm, jrm, bias_t, jbias_t, dstbuf, tb1, waits,
                            ct=ctil[sl], st=stil[sl]):
                    h_bc = lambda t: t[:].unsqueeze(1).broadcast_to((128, NH_B, DST))
                    b1_ = p.comp("vector", lambda e, rm=rm, bias_t=bias_t, dstbuf=dstbuf: e.tensor_tensor(
                        dstbuf[:].rearrange("p (h n) -> p h n", h=NH_B),
                        h_bc(rm), bias_t[:].rearrange("p (h n) -> p h n", h=NH_B),
                        ALU.add), waits=waits)
                    b2_ = p.comp("vector", lambda e, dstbuf=dstbuf, ct=ct: e.tensor_tensor(
                        dstbuf[:].rearrange("p (h n) -> p h n", h=NH_B),
                        dstbuf[:].rearrange("p (h n) -> p h n", h=NH_B),
                        h_bc(ct), ALU.mult), waits=[b1_, tc3])
                    b3_ = p.comp("vector", lambda e, jrm=jrm, jbias_t=jbias_t: e.tensor_tensor(
                        tb1[:].rearrange("p (h n) -> p h n", h=NH_B),
                        h_bc(jrm), jbias_t[:].rearrange("p (h n) -> p h n", h=NH_B),
                        ALU.add), waits=[b2_])
                    b4_ = p.comp("vector", lambda e, st=st: e.tensor_tensor(
                        tb1[:].rearrange("p (h n) -> p h n", h=NH_B),
                        tb1[:].rearrange("p (h n) -> p h n", h=NH_B),
                        h_bc(st), ALU.mult), waits=[b3_, ts3])
                    return p.comp("vector", lambda e, dstbuf=dstbuf: e.tensor_tensor(
                        dstbuf[:], dstbuf[:], tb1[:], ALU.add), waits=[b4_])

                bg = [prev_bst] if prev_bst else []
                cg = [prev_cst] if prev_cst else []
                if sp_tok:
                    bg.append(sp_tok)
                tBm = build_m(rmB_tm[sl], jrB_tm[sl], biasB_bc, jbB_bc,
                              bmt_buf[sl], cmt_one,
                              [trs["rB"], jr["B"], t_btabB, tc3, ts3] + bg + cg)
                t_bst = p.dma("sync", lambda e, cidx=cidx, sl=sl: e.dma_start(
                    out=bmt_out[cidx * 128:(cidx + 1) * 128, :], in_=bmt_buf[sl][:]),
                    waits=[tBm])
                prev_bst = t_bst
                w1_ = p.comp("vector", lambda e, ch=ch: e.tensor_tensor(
                    scr_tm[:], ltot_bc[:], la_tm[:, ch * 64:(ch + 1) * 64],
                    ALU.subtract), waits=[t_cum[(b, ch)], t_lb2, t_q4])
                w2_ = p.comp("scalar", lambda e: e.activation(
                    scr_tm[:], scr_tm[:], AF.Exp), waits=[w1_])
                w3_ = p.comp("vector", lambda e, sl=sl: e.tensor_tensor(
                    wt_bf[sl][:], scr_tm[:], gdt_tm[sl][:], ALU.mult),
                    waits=[w2_])
                xw = xw_buf[sl]
                w4_ = p.comp("vector", lambda e, sl=sl, xw=xw, xtb=xtb: e.tensor_tensor(
                    xw[:].rearrange("p (h q) -> p h q", h=NH_B),
                    xtb[:].rearrange("p (h q) -> p h q", h=NH_B),
                    wt_bf[sl][:].unsqueeze(2).broadcast_to((128, NH_B, HDIM)),
                    ALU.mult), waits=[w3_] + xev)
                for h2 in range(NH_B // 2):
                    mmw = [w4_, tBm]
                    if sp_tok:
                        mmw.append(sp_tok)
                    m1 = p.comp("tensor", lambda e, sl=sl, h2=h2, xw=xw: e.matmul(
                        ps_st[0:64, 0:HDIM],
                        bmt_buf[sl][:, (2 * h2) * DST:(2 * h2 + 1) * DST],
                        xw[:, (2 * h2) * HDIM:(2 * h2 + 1) * HDIM],
                        start=True, stop=True), waits=mmw)
                    m2 = p.comp("tensor", lambda e, sl=sl, h2=h2, xw=xw: e.matmul(
                        ps_st[0:64, HDIM:2 * HDIM],
                        bmt_buf[sl][:, (2 * h2 + 1) * DST:(2 * h2 + 2) * DST],
                        xw[:, (2 * h2 + 1) * HDIM:(2 * h2 + 2) * HDIM],
                        start=True, stop=True), waits=[])
                    if ch == 0:
                        sp_tok = p.comp("vector", lambda e, b=b, h2=h2: e.tensor_copy(
                            spack[b][:, (2 * h2) * HDIM:(2 * h2 + 2) * HDIM],
                            ps_st[0:64, 0:2 * HDIM]), waits=[m2])
                    else:
                        sp_tok = p.comp("vector", lambda e, b=b, h2=h2: e.tensor_tensor(
                            spack[b][:, (2 * h2) * HDIM:(2 * h2 + 2) * HDIM],
                            spack[b][:, (2 * h2) * HDIM:(2 * h2 + 2) * HDIM],
                            ps_st[0:64, 0:2 * HDIM], ALU.add), waits=[m2])
                tCm = build_m(rmC_tm[sl], jrC_tm[sl], biasC_bc, jbC_bc,
                              cmt_buf[sl], bmt_one,
                              [trs["rC"], jr["C"], t_btabC, tBm, t_bst, sp_tok])
                t_cst2 = p.dma("sync", lambda e, cidx=cidx, sl=sl: e.dma_start(
                    out=cmt_out[cidx * 128:(cidx + 1) * 128, :], in_=cmt_buf[sl][:]),
                    waits=[tCm])
                prev_cst = t_cst2
                prev_bst = tCm
            t_bcd = p.dma("sync", lambda e, b=b: e.dma_start(
                out=bc_out[b * DST:(b + 1) * DST, :], in_=spack[b][:]),
                waits=[sp_tok])
            prev_batch = [t_ac, t_lb2, ac_toks[-1], t_bcd]

        p.raw("sync", lambda e: e.nop(), waits=p.barrier_tokens())
        p.emit()
    return nc


def build_launch_G2():
    nc = new_nc()
    x_in = nc.dram_tensor("x", [TLOC, D], F32, kind="ExternalInput").ap()
    z_in = nc.dram_tensor("zi", [DIN, TLOC], BF16, kind="ExternalInput").ap()
    xfm_in = nc.dram_tensor("xfi", [DIN, TLOC], BF16, kind="ExternalInput").ap()
    xtm_in = nc.dram_tensor("xti", [TLOC, DIN], BF16, kind="ExternalInput").ap()
    la_in = nc.dram_tensor("lai", [NH_B, TLOC], F32, kind="ExternalInput").ap()
    gdt_in = nc.dram_tensor("gdi", [NH_B, TLOC], F32, kind="ExternalInput").ap()
    bmt_in = nc.dram_tensor("bmi", [TLOC, NH_B * DST], BF16, kind="ExternalInput").ap()
    cmt_in = nc.dram_tensor("cmi", [TLOC, NH_B * DST], BF16, kind="ExternalInput").ap()
    ac_in = nc.dram_tensor("aci", [N_CORES * B, 2048], F32, kind="ExternalInput").ap()
    bc_in = nc.dram_tensor("bci", [N_CORES * B * 128, 1024], F32, kind="ExternalInput").ap()
    msk_in = nc.dram_tensor("mski", [1, N_CORES], F32, kind="ExternalInput").ap()
    dexp_in = nc.dram_tensor("dxi", [128, 16], F32, kind="ExternalInput").ap()
    wo_in = nc.dram_tensor("wo", [D, DIN], F32, kind="ExternalInput").ap()
    y_out = nc.dram_tensor("y", [TLOC, D], F32, kind="ExternalOutput").ap()

    stage_wo = nc.dram_tensor("stgwo", [D, DIN], BF16, kind="Internal").ap()

    with ExitStack() as ctx, Prog(nc) as p:
        E = ctx.enter_context
        woT = [sb(E, nc, f"wot{k}", [128, D], BF16) for k in range(16)]
        cbb = [sb(E, nc, f"cbb{i}", [128, DIN], BF16) for i in range(2)]
        la_fm = sb(E, nc, "lafm", [64, TLOC], F32)
        gdt_fm = sb(E, nc, "gdfm", [64, TLOC], F32)
        h32 = [sb(E, nc, f"h32_{b}", [128, 1024], F32) for b in range(B)]
        hbf = [sb(E, nc, f"hbf_{b}", [128, 1024], BF16) for b in range(B)]
        abc = sb(E, nc, "abc", [128, 1024], F32)
        bld = sb(E, nc, "bld", [128, 1024], F32)
        arow = sb(E, nc, "arow", [1, 2048], F32)
        mrow = sb(E, nc, "mrow", [1, N_CORES], F32)
        m_bc = sb(E, nc, "mbc", [128, 2 * N_CORES], F32)
        bmt_c = [sb(E, nc, f"bmc{i}", [128, NH_B * DST], BF16) for i in range(2)]
        cmt_c = [sb(E, nc, f"cmc{i}", [128, NH_B * DST], BF16) for i in range(2)]
        cdec = sb(E, nc, "cdec", [128, 128], BF16)
        xtm_c = [sb(E, nc, f"xtc{i}", [128, DIN], BF16) for i in range(2)]
        xwI = sb(E, nc, "xwi", [128, DIN], BF16)
        xw2 = sb(E, nc, "xw2", [128, DIN], BF16)
        lsum = sb(E, nc, "lsum", [64, 130], F32)
        ls_row = sb(E, nc, "lsrow", [1, 64], F32)
        S_bc = sb(E, nc, "sbc", [128, 64], F32)
        scr2 = sb(E, nc, "scr2", [128, 64], F32)
        Abc64 = sb(E, nc, "abc64", [128, 64], F32)
        xfc = [sb(E, nc, f"xfc{i}", [128, DIN], BF16) for i in range(2)]
        zc = sb(E, nc, "zc", [128, DIN], BF16)
        yc = [sb(E, nc, f"yc{k}", [128, 128], BF16) for k in range(16)]
        szb = sb(E, nc, "szb", [128, 128], BF16)
        gfm = [sb(E, nc, f"gfm{i}", [128, 128], BF16) for i in range(2)]
        cfm = [sb(E, nc, f"cfm{i}", [128, 128], BF16) for i in range(2)]
        cdfm = [sb(E, nc, f"cdf{i}", [128, 128], BF16) for i in range(2)]
        sdb = [sb(E, nc, f"sdb{i}", [128, 128], BF16) for i in range(2)]
        dbuf = [sb(E, nc, f"dbf{i}", [128, 128], F32) for i in range(2)]
        cum_tm = [sb(E, nc, f"cumt{i}", [128, 64], F32) for i in range(2)]
        ncum = [sb(E, nc, f"ncum{i}", [128, 64], F32) for i in range(2)]
        ecum = [sb(E, nc, f"ecum{i}", [128, 64], F32) for i in range(2)]
        gdt_bf = [sb(E, nc, f"gdb{i}", [128, 64], BF16) for i in range(2)]
        scr_tm = sb(E, nc, "scrtm", [128, 64], F32)
        crow = sb(E, nc, "crow", [1, 128], F32)
        x_sb = [sb(E, nc, f"x{i}", [128, D], F32) for i in range(2)]
        o_sb = [sb(E, nc, f"o{i}", [128, D], F32) for i in range(2)]
        ones_row = sb(E, nc, "onesr", [1, 128], F32)
        maskU = sb(E, nc, "masku", [128, 128], F32)
        trilI = sb(E, nc, "trilI", [128, 128], F32)
        identf = sb(E, nc, "identf", [128, 128], F32)
        dexp = sb(E, nc, "dexp", [128, 16], F32)
        ident, t_id = build_identity(p, nc, E)
        cpool = ConstPool(p, nc, E)

        ps_tp = [ps(E, nc, f"pst{i}", [128, 128], BF16) for i in range(2)]
        psS = ps(E, nc, "psS", [128, 128], F32)
        psD = ps(E, nc, "psD", [128, 128], F32)
        psY = [ps(E, nc, f"psY{i}", [128, 128], F32) for i in range(2)]
        ps_o = [ps(E, nc, f"pso{i}", [128, 512], F32) for i in range(2)]

        t_or = p.comp("gpsimd", lambda e: e.memset(ones_row[:], 1.0))
        t_mk0 = p.comp("gpsimd", lambda e: e.memset(maskU[:], 1.0))
        t_mk = p.comp("gpsimd", lambda e: e.affine_select(
            maskU[:], maskU[:], pattern=[[1, 128]], compare_op=ALU.is_ge,
            fill=0.0, base=0, channel_multiplier=-1), waits=[t_mk0])
        t_ti0 = p.comp("gpsimd", lambda e: e.memset(trilI[:], 1.0))
        t_ti = p.comp("gpsimd", lambda e: e.affine_select(
            trilI[:], trilI[:], pattern=[[1, 128]], compare_op=ALU.is_ge,
            fill=0.0, base=0, channel_multiplier=-1), waits=[t_ti0])
        t_if0 = p.comp("gpsimd", lambda e: e.memset(identf[:], 1.0))
        t_if = p.comp("gpsimd", lambda e: e.affine_select(
            identf[:], identf[:], pattern=[[1, 128]], compare_op=ALU.is_equal,
            fill=0.0, base=0, channel_multiplier=-1), waits=[t_if0])

        t_la = p.dma("sync", lambda e: e.dma_start(out=la_fm[:], in_=la_in))
        t_gd = p.dma("sync", lambda e: e.dma_start(out=gdt_fm[:], in_=gdt_in))
        t_dx = p.dma("sync", lambda e: e.dma_start(out=dexp[:], in_=dexp_in))
        t_ms = p.dma("sync", lambda e: e.dma_start(out=mrow[:], in_=msk_in))
        t_mb0 = p.comp("tensor", lambda e: e.matmul(
            psD[0:128, 0:N_CORES], ones_row[:], mrow[:], start=True, stop=True),
            waits=[t_ms, t_or])
        t_mb1 = p.comp("vector", lambda e: e.tensor_copy(
            m_bc[:, 0:N_CORES], psD[0:128, 0:N_CORES]), waits=[t_mb0])
        t_mb2 = p.comp("vector", lambda e: e.tensor_scalar(
            m_bc[:, N_CORES:2 * N_CORES], m_bc[:, 0:N_CORES], -1.0, 1.0,
            ALU.mult, ALU.add), waits=[t_mb1])

        t_c = []
        for r in range(0, D, 128):
            bufi = (r // 128) % 2
            lw = [t_c[-2][1]] if len(t_c) >= 2 else []
            t_l = p.dma("gpsimd", lambda e, r=r, bufi=bufi: e.dma_start(
                out=cbb[bufi][:], in_=wo_in[r:r + 128, :]), waits=lw)
            t_s = p.dma("scalar", lambda e, r=r, bufi=bufi: e.dma_start(
                out=stage_wo[r:r + 128, :], in_=cbb[bufi][:]), waits=[t_l])
            t_c.append((t_l, t_s))
        t_wo = []
        for kc in range(16):
            for db in range(2):
                t_wo.append(p.dma("sync", lambda e, kc=kc, db=db: e.dma_start(
                    out=woT[kc][:, db * 512:(db + 1) * 512],
                    in_=stage_wo[db * 512:(db + 1) * 512,
                                 kc * 128:(kc + 1) * 128], transpose=True),
                    waits=[t[1] for t in t_c]))
        t_woL = t_wo[-1]

        t_h = {}
        prev = None
        for b in range(B):
            for cpre in range(N_CORES):
                lw = [prev] if prev else []
                ta1 = p.dma("sync", lambda e, b=b, cpre=cpre: e.dma_start(
                    out=arow[0:1, :],
                    in_=ac_in[cpre * B + b:cpre * B + b + 1, :]), waits=lw)
                ta2 = ta1
                tes = []
                for par in range(2):
                    for hh in range(2):
                        bko_ = ps_o[hh]
                        tm_ = p.comp("tensor", lambda e, par=par, hh=hh, bko_=bko_: e.matmul(
                            bko_[par * 64:(par + 1) * 64, :],
                            ones_row[0:1, 0:64],
                            arow[0:1, par * 1024 + hh * 512:
                                 par * 1024 + (hh + 1) * 512],
                            start=True, stop=True,
                            tile_position=(0, par * 64)),
                            waits=[ta1, t_mb2] + (tes[-2:-1] if len(tes) >= 2 else []))
                        te_ = p.comp("vector", lambda e, par=par, hh=hh, bko_=bko_: e.tensor_copy(
                            abc[par * 64:(par + 1) * 64, hh * 512:(hh + 1) * 512],
                            bko_[par * 64:(par + 1) * 64, :]), waits=[tm_])
                        tes.append(te_)
                tap = p.comp("vector", lambda e, cpre=cpre: e.tensor_scalar(
                    abc[:], abc[:], m_bc[:, cpre:cpre + 1],
                    m_bc[:, N_CORES + cpre:N_CORES + cpre + 1],
                    ALU.mult, ALU.add), waits=tes + [t_mb2])
                tbl = p.dma("sync", lambda e, b=b, cpre=cpre: e.dma_start(
                    out=bld[:], in_=bc_in[(cpre * B + b) * 128:
                                          (cpre * B + b + 1) * 128, :]),
                    waits=lw)
                if cpre == 0:
                    th1 = p.comp("vector", lambda e, b=b, cpre=cpre: e.tensor_scalar(
                        h32[b][:], bld[:], m_bc[:, cpre:cpre + 1], None,
                        ALU.mult), waits=[tbl, t_mb2] + ([prev] if prev else []))
                else:
                    th0 = p.comp("vector", lambda e, b=b: e.tensor_tensor(
                        h32[b][:], h32[b][:], abc[:], ALU.mult),
                        waits=[tap] + ([prev] if prev else []))
                    th1 = p.comp("vector", lambda e, b=b, cpre=cpre: e.scalar_tensor_tensor(
                        h32[b][:], bld[:], m_bc[:, cpre:cpre + 1], h32[b][:],
                        ALU.mult, ALU.add), waits=[th0, tbl])
                t_h[(b, cpre)] = th1
                prev = th1
            t_h[b] = p.comp("vector", lambda e, b=b: e.tensor_copy(
                hbf[b][:], h32[b][:]), waits=[prev])
            prev = t_h[b]


        last = {}
        last["bko"] = t_h[B - 1]
        last["bko2"] = t_h[B - 1]
        last[("pso", 0)] = t_h[B - 1]
        last[("pso", 1)] = t_h[B - 1]
        t_hb = {b: t_h[b] for b in range(B)}
        t_y = []
        slot_g = {}
        for b in range(B):
            for ch in range(8):
                cidx = b * 8 + ch
                col0 = b * SLOC + ch * 128
                sl = cidx % 2
                gl = slot_g.get(sl, [])
                t_bm = p.dma("sync", lambda e, cidx=cidx, sl=sl: e.dma_start(
                    out=bmt_c[sl][:], in_=bmt_in[cidx * 128:(cidx + 1) * 128, :]),
                    waits=gl)
                t_cm = p.dma("scalar", lambda e, cidx=cidx, sl=sl: e.dma_start(
                    out=cmt_c[sl][:], in_=cmt_in[cidx * 128:(cidx + 1) * 128, :]),
                    waits=gl)
                t_xt = p.dma("sync", lambda e, cidx=cidx, sl=sl: e.dma_start(
                    out=xtm_c[sl][:], in_=xtm_in[cidx * 128:(cidx + 1) * 128, :]),
                    waits=gl)
                t_xf = []
                for kc in range(16):
                    t_xf.append(p.dma("scalar", lambda e, kc=kc, col0=col0, sl=sl: e.dma_start(
                        out=xfc[sl][:, kc * 128:(kc + 1) * 128],
                        in_=xfm_in[kc * 128:(kc + 1) * 128, col0:col0 + 128]),
                        waits=gl))
                bko = ps_o[0]
                w0 = [t_la, t_if, last.get("bko"), last.get(("pso", 0))]
                tq1 = p.comp("tensor", lambda e, col0=col0, bko=bko: e.transpose(
                    bko[0:128, 0:64], la_fm[:, col0:col0 + 128],
                    identf[0:64, 0:64]), waits=[w for w in w0 if w])
                tq2 = p.comp("vector", lambda e, bko=bko: e.tensor_copy(
                    scr_tm[:], bko[0:128, 0:64]), waits=[tq1])
                tq3 = p.comp("tensor", lambda e, bko=bko: e.matmul(
                    bko[0:128, 64:128], trilI[:], scr_tm[:],
                    start=True, stop=True), waits=[tq2, t_ti])
                tq4 = p.comp("vector", lambda e, sl=sl, bko=bko: e.tensor_copy(
                    cum_tm[sl][:], bko[0:128, 64:128]), waits=[tq3])
                last["bko"] = tq4
                tq5 = p.comp("vector", lambda e, sl=sl: e.tensor_scalar(
                    ncum[sl][:], cum_tm[sl][:], -1.0, None, ALU.mult),
                    waits=[tq4])
                tq6 = p.comp("scalar", lambda e, sl=sl: e.activation(
                    ecum[sl][:], cum_tm[sl][:], AF.Exp), waits=[tq5])
                tS1 = p.comp("scalar", lambda e, col0=col0: e.activation(
                    lsum[:, 0:128], la_fm[:, col0:col0 + 128], AF.Copy,
                    bias=0.0, scale=1.0, accum_out=lsum[:, 128:129]),
                    waits=[t_la, last.get("lsum")] if last.get("lsum")
                    else [t_la])
                tS2 = p.comp("tensor", lambda e: e.transpose(
                    psD[0:1, 0:64], lsum[:, 128:129], identf[0:64, 0:64]),
                    waits=[tS1, t_if] + ([last[("psD", 0)]]
                                         if ("psD", 0) in last else []))
                tS3 = p.comp("vector", lambda e: e.tensor_copy(
                    ls_row[:], psD[0:1, 0:64]), waits=[tS2])
                last["lsum"] = tS2
                tS4 = p.comp("tensor", lambda e: e.matmul(
                    psD[0:128, 0:64], ones_row[:], ls_row[:],
                    start=True, stop=True), waits=[tS3, t_or])
                tS5 = p.comp("vector", lambda e: e.tensor_copy(
                    S_bc[:], psD[0:128, 0:64]), waits=[tS4])
                last[("psD", 0)] = tS5
                tS6 = p.comp("vector", lambda e, sl=sl: e.tensor_tensor(
                    scr2[:], S_bc[:], cum_tm[sl][:], ALU.subtract),
                    waits=[tS5, tq4])
                tS7 = p.comp("scalar", lambda e: e.activation(
                    scr2[:], scr2[:], AF.Exp), waits=[tS6])
                tS8 = p.comp("scalar", lambda e: e.activation(
                    Abc64[:], S_bc[:], AF.Exp),
                    waits=[tS5] + ([last["abc64"]] if "abc64" in last else []))
                w0 = [t_gd, t_if, last.get("bko2")]
                tq7 = p.comp("tensor", lambda e, col0=col0, bko=bko: e.transpose(
                    bko[0:128, 0:64], gdt_fm[:, col0:col0 + 128],
                    identf[0:64, 0:64]), waits=[w for w in w0 if w] + [tq2])
                tq8 = p.comp("vector", lambda e, sl=sl, bko=bko: e.tensor_copy(
                    gdt_bf[sl][:], bko[0:128, 0:64]), waits=[tq7])
                last["bko2"] = tq8
                tq9 = p.comp("vector", lambda e, sl=sl: e.tensor_tensor(
                    xwI[:].rearrange("p (h q) -> p h q", h=NH_B),
                    xtm_c[sl][:].rearrange("p (h q) -> p h q", h=NH_B),
                    gdt_bf[sl][:].unsqueeze(2).broadcast_to((128, NH_B, HDIM)),
                    ALU.mult), waits=[tq8, t_xt, last.get("xwI")])
                t_xw2 = p.comp("vector", lambda e: e.tensor_tensor(
                    xw2[:].rearrange("p (h q) -> p h q", h=NH_B),
                    xwI[:].rearrange("p (h q) -> p h q", h=NH_B),
                    scr2[:].unsqueeze(2).broadcast_to((128, NH_B, HDIM)),
                    ALU.mult), waits=[tq9, tS7, last.get("xw2")])
                last["xwI"] = t_xw2

                yev = []
                for np_ in range(32):
                    tsl = np_ % 2
                    ptp = ps_tp[tsl]
                    w0 = [t_bm, t_id, last.get(("ptp", tsl))]
                    tp1 = p.comp("tensor", lambda e, np_=np_, sl=sl, ptp=ptp: e.transpose(
                        ptp[:], bmt_c[sl][:, np_ * 128:(np_ + 1) * 128], ident[:]),
                        waits=[w for w in w0 if w])
                    tp2 = p.comp("vector", lambda e, tsl=tsl, ptp=ptp: e.tensor_copy(
                        gfm[tsl][:], ptp[:]), waits=[tp1])
                    tp3 = p.comp("tensor", lambda e, np_=np_, sl=sl, ptp=ptp: e.transpose(
                        ptp[:], cmt_c[sl][:, np_ * 128:(np_ + 1) * 128], ident[:]),
                        waits=[tp2, t_cm])
                    tp4 = p.comp("vector", lambda e, tsl=tsl, ptp=ptp: e.tensor_copy(
                        cfm[tsl][:], ptp[:]), waits=[tp3])
                    tc1 = p.comp("vector", lambda e, np_=np_, sl=sl: e.tensor_scalar(
                        cdec[:, 0:64], cmt_c[sl][:, np_ * 128:np_ * 128 + 64],
                        ecum[sl][:, 2 * np_:2 * np_ + 1], None, ALU.mult),
                        waits=[tq6, t_cm, last.get("cdec")])
                    tc2 = p.comp("vector", lambda e, np_=np_, sl=sl: e.tensor_scalar(
                        cdec[:, 64:128], cmt_c[sl][:, np_ * 128 + 64:(np_ + 1) * 128],
                        ecum[sl][:, 2 * np_ + 1:2 * np_ + 2], None, ALU.mult),
                        waits=[tc1])
                    tp5 = p.comp("tensor", lambda e, ptp=ptp: e.transpose(
                        ptp[:], cdec[:], ident[:]), waits=[tc2, tp4])
                    tp6 = p.comp("vector", lambda e, tsl=tsl, ptp=ptp: e.tensor_copy(
                        cdfm[tsl][:], ptp[:]), waits=[tp5])
                    last[("ptp", tsl)] = tp6
                    last["cdec"] = tp5
                    for par in range(2):
                        n = 2 * np_ + par
                        r0 = par * 64
                        dsl = n % 2
                        w0 = [tq4, last.get(("psD", 0))]
                        td1 = p.comp("tensor", lambda e, n=n, sl=sl: e.transpose(
                            psD[0:1, 0:128], cum_tm[sl][:, n:n + 1],
                            identf[:]), waits=[w for w in w0 if w])
                        td2 = p.comp("vector", lambda e: e.tensor_copy(
                            crow[:], psD[0:1, 0:128]), waits=[td1])
                        td3 = p.comp("tensor", lambda e: e.matmul(
                            psD[:], ones_row[:], crow[:], start=True, stop=True),
                            waits=[td2])
                        td4 = p.comp("scalar", lambda e, n=n, sl=sl, dsl=dsl: e.activation(
                            dbuf[dsl][:], psD[:], AF.Exp,
                            bias=ncum[sl][:, n:n + 1], scale=1.0),
                            waits=[td3, tq5, last.get(("dbuf", dsl))])
                        last[("psD", 0)] = td4
                        td5 = p.comp("vector", lambda e, dsl=dsl: e.tensor_tensor(
                            dbuf[dsl][:], dbuf[dsl][:], maskU[:], ALU.mult),
                            waits=[td4, t_mk])
                        w1 = [tp2, tp4, last.get(("psS", 0))]
                        ts1 = p.comp("tensor", lambda e, tsl=tsl, r0=r0: e.matmul(
                            psS[:], gfm[tsl][r0:r0 + 64, :], cfm[tsl][r0:r0 + 64, :],
                            start=True, stop=True, tile_position=(r0, 0)),
                            waits=[w for w in w1 if w])
                        ts2 = p.comp("vector", lambda e, dsl=dsl: e.tensor_tensor(
                            sdb[dsl][:], psS[:], dbuf[dsl][:], ALU.mult),
                            waits=[ts1, td5, last.get(("sdb", dsl))])
                        last[("psS", 0)] = ts2
                        last[("dbuf", dsl)] = ts2
                        yps = psY[(n // 2) % 2]
                        rY = (n % 4) * 32
                        w2 = [ts2, tq9, last.get(("psY", (n // 2) % 2))]
                        ty1 = p.comp("tensor", lambda e, n=n, rY=rY, dsl=dsl, yps=yps: e.matmul(
                            yps[rY:rY + 32, :], xwI[:, n * HDIM:(n + 1) * HDIM],
                            sdb[dsl][:], start=True, stop=False,
                            tile_position=(0, rY)), waits=[w for w in w2 if w])
                        ty2 = p.comp("tensor", lambda e, b=b, n=n, rY=rY, r0=r0, tsl=tsl, yps=yps: e.matmul(
                            yps[rY:rY + 32, :],
                            hbf[b][r0:r0 + 64, (n // 2) * HDIM:(n // 2 + 1) * HDIM],
                            cdfm[tsl][r0:r0 + 64, :], start=False, stop=True,
                            tile_position=(r0, rY)), waits=[tp6, t_hb[b]])
                        kc = n // 4
                        tyev = p.comp("vector", lambda e, n=n, kc=kc, rY=rY, sl=sl, yps=yps: e.scalar_tensor_tensor(
                            yc[kc][rY:rY + 32, :],
                            xfc[sl][rY:rY + 32, kc * 128:(kc + 1) * 128],
                            dexp[rY:rY + 32, kc:kc + 1], yps[rY:rY + 32, :],
                            ALU.mult, ALU.add),
                            waits=[ty1, ty2, t_xf[kc], t_dx,
                                   last.get(("yc", n % 8))])
                        last[("psY", (n // 2) % 2)] = tyev
                        last[("yc", n % 8)] = tyev
                        yev.append(tyev)
                stt_prev = []
                for np_ in range(32):
                    mw = [t_xw2, t_bm] + stt_prev
                    if np_ == 0:
                        mw.append(last[("psD", 0)])
                    mS1 = p.comp("tensor", lambda e, np_=np_, sl=sl: e.matmul(
                        psD[0:64, 0:HDIM],
                        bmt_c[sl][:, (2 * np_) * DST:(2 * np_ + 1) * DST],
                        xw2[:, (2 * np_) * HDIM:(2 * np_ + 1) * HDIM],
                        start=True, stop=True, tile_position=(0, 0)),
                        waits=mw)
                    mS2 = p.comp("tensor", lambda e, np_=np_, sl=sl: e.matmul(
                        psD[64:128, 0:HDIM],
                        bmt_c[sl][:, (2 * np_ + 1) * DST:(2 * np_ + 2) * DST],
                        xw2[:, (2 * np_ + 1) * HDIM:(2 * np_ + 2) * HDIM],
                        start=True, stop=True, tile_position=(0, 64)),
                        waits=mw)
                    st1 = p.comp("vector", lambda e, b=b, np_=np_: e.scalar_tensor_tensor(
                        h32[b][0:64, np_ * HDIM:(np_ + 1) * HDIM],
                        h32[b][0:64, np_ * HDIM:(np_ + 1) * HDIM],
                        Abc64[0:64, 2 * np_:2 * np_ + 1],
                        psD[0:64, 0:HDIM], ALU.mult, ALU.add),
                        waits=[mS1, tS8])
                    st2 = p.comp("vector", lambda e, b=b, np_=np_: e.scalar_tensor_tensor(
                        h32[b][64:128, np_ * HDIM:(np_ + 1) * HDIM],
                        h32[b][64:128, np_ * HDIM:(np_ + 1) * HDIM],
                        Abc64[64:128, 2 * np_ + 1:2 * np_ + 2],
                        psD[64:128, 0:HDIM], ALU.mult, ALU.add),
                        waits=[mS2, tS8])
                    stt_prev = [st1, st2]
                last[("psD", 0)] = stt_prev[-1]
                last["abc64"] = stt_prev[-1]
                last["xw2"] = stt_prev[-1]
                t_hbc = p.comp("vector", lambda e, b=b: e.tensor_copy(
                    hbf[b][:], h32[b][:]), waits=stt_prev)
                t_hb[b] = t_hbc
                t_zl = p.dma("sync", lambda e, col0=col0: e.dma_start(
                    out=zc[:], in_=z_in[0:DIN, col0:col0 + 128]
                    .rearrange("(a p) t -> p (a t)", p=128)), waits=gl) \
                    if False else None
                t_zs = []
                for kc in range(16):
                    t_zs.append(p.dma("sync", lambda e, kc=kc, col0=col0: e.dma_start(
                        out=zc[:, kc * 128:(kc + 1) * 128],
                        in_=z_in[kc * 128:(kc + 1) * 128, col0:col0 + 128]),
                        waits=gl + ([last["zc"]] if "zc" in last else [])))
                guard2 = []
                t_xr = p.dma("sync", lambda e, cidx=cidx, sl=sl: e.dma_start(
                    out=x_sb[sl][:], in_=x_in[cidx * 128:(cidx + 1) * 128, :]),
                    waits=gl)
                mm_toks = []
                for kc in range(16):
                    tsz = p.comp("scalar", lambda e, kc=kc: e.activation(
                        szb[:], zc[:, kc * 128:(kc + 1) * 128], AF.Silu),
                        waits=[t_zs[kc], yev[-1]] + ([mm_toks[-1]] if mm_toks else []))
                    tyg = p.comp("vector", lambda e, kc=kc: e.tensor_tensor(
                        yc[kc][:], yc[kc][:], szb[:], ALU.mult), waits=[tsz])
                    if kc == 15:
                        last["zc"] = tsz
                    for h in range(2):
                        bk = ps_o[h]
                        w3 = [tyg, t_woL]
                        if kc == 0:
                            w3.append(last.get(("pso", h)))
                            if h == 0:
                                w3.append(last.get("bko"))
                                w3.append(last.get("bko2"))
                        mm = p.comp("tensor", lambda e, kc=kc, h=h, bk=bk: e.matmul(
                            bk[:], yc[kc][:], woT[kc][:, h * 512:(h + 1) * 512],
                            start=(kc == 0), stop=(kc == 15)),
                            waits=[w for w in w3 if w])
                    mm_toks.append(mm)
                evs = []
                for h in range(2):
                    bk = ps_o[h]
                    t_e = p.comp("vector", lambda e, h=h, sl=sl, bk=bk: e.tensor_tensor(
                        o_sb[sl][:, h * 512:(h + 1) * 512], bk[:],
                        x_sb[sl][:, h * 512:(h + 1) * 512], ALU.add),
                        waits=[mm_toks[-1], t_xr, last.get(("osb", sl))])
                    last[("pso", h)] = t_e
                    evs.append(t_e)
                t_st = p.dma("sync", lambda e, cidx=cidx, sl=sl: e.dma_start(
                    out=y_out[cidx * 128:(cidx + 1) * 128, :], in_=o_sb[sl][:]),
                    waits=evs)
                last[("osb", sl)] = t_st
                t_y.append(t_st)
                slot_g[sl] = [t_st, yev[-1], mm_toks[-1], t_hbc]
        p.raw("sync", lambda e: e.nop(), waits=p.barrier_tokens())
        p.emit()
    return nc


def run_stage_G(x_shards, w_in, dt_bias, b_bias, c_bias, bn_w, cn_w, d_vec, w_out):
    nca = cached_nc("G1a", build_launch_G1a)
    dtb0 = dt_bias.reshape(1, NH_B)
    bnw0 = bn_w.reshape(1, DST)
    cnw0 = cn_w.reshape(1, DST)
    res_a = run_bass_kernel_spmd(
        nca, [dict(x=x_shards[c], win=w_in, dtb=dtb0, bb=b_bias, cb=c_bias,
                   bnw=bnw0, cnw=cnw0) for c in range(N_CORES)],
        core_ids=list(range(N_CORES)))
    oa = res_a.results
    ncb = cached_nc("G1b", build_launch_G1b)
    dtb = dt_bias.reshape(1, NH_B)
    bnw = bn_w.reshape(1, DST)
    cnw = cn_w.reshape(1, DST)
    res_b = run_bass_kernel_spmd(
        ncb, [dict(smi=oa[c]["smo"], xti=oa[c]["xto"], dtb=dtb, bb=b_bias,
                   cb=c_bias, bnw=bnw, cnw=cnw) for c in range(N_CORES)],
        core_ids=list(range(N_CORES)))
    ob = res_b.results
    acall = np.concatenate([ob[c]["aco"] for c in range(N_CORES)], axis=0)
    ac_exp = np.empty((acall.shape[0], 2048), np.float32)
    for par in range(2):
        ac_exp[:, par * 1024:(par + 1) * 1024] = np.repeat(
            acall[:, par::2], HDIM, axis=1)
    bcs = []
    for c in range(N_CORES):
        bc = ob[c]["bco"].reshape(B, DST, NH_B, HDIM)
        pk = np.zeros((B, 128, 1024), np.float32)
        for par in range(2):
            pk[:, par * 64:(par + 1) * 64, :] = bc[:, :, par::2, :].reshape(
                B, DST, 1024)
        bcs.append(pk.reshape(B * 128, 1024))
    bcall = np.concatenate(bcs, axis=0)
    dexp = np.repeat(d_vec.reshape(16, 4), 32, axis=1).reshape(16, 128).T
    dexp = np.ascontiguousarray(dexp, np.float32)
    ncg = cached_nc("G2", build_launch_G2)
    in2 = []
    for c in range(N_CORES):
        mask = np.zeros((1, N_CORES), np.float32)
        mask[0, :c] = 1.0
        in2.append(dict(x=x_shards[c], zi=ob[c]["zo"] if False else oa[c]["zo"],
                        xfi=oa[c]["xfo"], xti=oa[c]["xto"], lai=ob[c]["lao"],
                        gdi=ob[c]["gdo"], bmi=ob[c]["bmo"], cmi=ob[c]["cmo"],
                        aci=ac_exp, bci=bcall, mski=mask, dxi=dexp, wo=w_out))
    res_g = run_bass_kernel_spmd(ncg, in2, core_ids=list(range(N_CORES)))
    return [res_g.results[c]["y"] for c in range(N_CORES)]


def run_stage_C(x_shards, router_w, router_b, w1, b1, w2, b2):
    ncs1 = cached_nc("C1", build_launch_C1)
    res1 = run_bass_kernel_spmd(
        ncs1, [dict(x=x_shards[c], rw=router_w) for c in range(N_CORES)],
        core_ids=list(range(N_CORES)))
    o1 = res1.results
    xh_full = np.concatenate([o1[c]["xh"].reshape(B, SLOC, D)
                              for c in range(N_CORES)], axis=1).reshape(NTOK, D)
    lg_full = np.concatenate([o1[c]["lg"].reshape(B, SLOC, NEXP)
                              for c in range(N_CORES)], axis=1).reshape(NTOK, NEXP)
    rb = router_b.reshape(1, NEXP).astype(np.float32)
    ncs2 = cached_nc("C2", build_launch_C2)
    in2 = []
    for e in range(N_CORES):
        esel = np.zeros((1, NEXP), np.float32)
        esel[0, e] = 1.0
        in2.append(dict(xhf=xh_full, lgf=lg_full, rb=rb, esel=esel,
                        w1=np.ascontiguousarray(w1[e]),
                        b1c=np.ascontiguousarray(b1[e].reshape(32, 128).T),
                        w2=np.ascontiguousarray(w2[e]),
                        b2c=np.ascontiguousarray(b2[e].reshape(8, 128).T)))
    res2 = run_bass_kernel_spmd(ncs2, in2, core_ids=list(range(N_CORES)))
    o2 = res2.results
    base = o2[0]["base"]
    oall = np.concatenate([o2[e]["oute"] for e in range(N_CORES)], axis=0)
    oall = np.concatenate([oall, np.zeros((1, D), oall.dtype)], axis=0)
    ncs3 = cached_nc("C3", build_launch_C3)
    in3 = []
    for c in range(N_CORES):
        gcs = [bb * 64 + c * 8 + j for bb in range(B) for j in range(8)]
        bmy = np.ascontiguousarray(base[gcs].reshape(1, NCH * NEXP))
        in3.append(dict(x=x_shards[c], lgmy=o1[c]["lg"], rb=rb, bmy=bmy,
                        oall=oall))
    res3 = run_bass_kernel_spmd(ncs3, in3, core_ids=list(range(N_CORES)))
    return [res3.results[c]["y"] for c in range(N_CORES)]


def run_stage_D(x_shards, wv, bv):
    ncd = cached_nc("D", build_launch_D)
    res = run_bass_kernel_spmd(
        ncd, [dict(x=x_shards[c], wv=wv, bv=bv.reshape(1, D)) for c in range(N_CORES)],
        core_ids=list(range(N_CORES)))
    return [res.results[c]["y"] for c in range(N_CORES)]


def run_stage_F(x_shards, gate_w, weight):
    ncf = cached_nc("F", build_launch_F)
    we = np.ascontiguousarray(weight.reshape(NEXP * D, D))
    res = run_bass_kernel_spmd(
        ncf, [dict(x=x_shards[c], gw=gate_w, we=we) for c in range(N_CORES)],
        core_ids=list(range(N_CORES)))
    return [res.results[c]["y"] for c in range(N_CORES)]


def kernel(**inputs):
    f32 = lambda k: np.ascontiguousarray(np.asarray(inputs[k]), np.float32)
    x = f32("L0_attn_a_x")
    xs = shard_x(x)
    outA = run_stage_A(xs, f32("L0_attn_a_W_query"), f32("L0_attn_a_W_key"),
                       f32("L0_attn_a_W_value"))
    xs = run_stage_B(outA, xs)
    moe_args = (f32("L0_ffn_router_w"), f32("L0_ffn_router_b"),
                f32("L0_ffn_w1"), f32("L0_ffn_b1"),
                f32("L0_ffn_w2"), f32("L0_ffn_b2"))
    xs = run_stage_C(xs, *moe_args)
    xs = run_stage_D(xs, f32("L0_attn_b_Wv"), f32("L0_attn_b_bv"))
    xs = run_stage_C(xs, *moe_args)
    xs = run_stage_E(xs, f32("L1_attn_a_W_proj"),
                     f32("L1_attn_a_pre_metric_nkk"),
                     f32("L1_attn_a_transforms_nkk"), f32("L1_attn_a_W_mixer"))
    ffn_args = (f32("L1_ffn_gate_w"), f32("L1_ffn_weight"))
    xs = run_stage_F(xs, *ffn_args)
    xs = run_stage_G(xs, f32("L1_attn_b_W_in"), f32("L1_attn_b_dt_bias"),
                     f32("L1_attn_b_B_bias"), f32("L1_attn_b_C_bias"),
                     f32("L1_attn_b_Bn_w"), f32("L1_attn_b_Cn_w"),
                     f32("L1_attn_b_D"), f32("L1_attn_b_W_out"))
    xs = run_stage_F(xs, *ffn_args)
    return unshard_x([np.asarray(s, np.float32) for s in xs])


if __name__ == "__main__":
    pass
